# version 01
# baseline (speedup 1.0000x reference)
import numpy as np
import concourse.bass as bass
import concourse.mybir as mybir
from concourse.bass_utils import run_bass_kernel_spmd

F32 = mybir.dt.float32
BF16 = mybir.dt.bfloat16
AF = mybir.ActivationFunctionType
ALU = mybir.AluOpType
AX = mybir.AxisListType.X

T = 4096
DM = 1024
DI = 2048
NST = 16
DTR = 64
FI = 2816
NH = 16
MAGIC = float(np.float32(3 << 22))
f32 = np.float32


class Prog:
    def __init__(self):
        self.ops = []

    def op(self, eng, fn):
        self.ops.append((eng, fn, False))

    def dma(self, queue, out, in_, **kw):
        self.ops.append((queue, lambda e: e.dma_start(out, in_, **kw), True))

    def dmat(self, queue, out, in_):
        self.ops.append((queue, lambda e: e.dma_start_transpose(out, in_), True))

    def t(self, fn): self.op("tensor", fn)
    def v(self, fn): self.op("vector", fn)
    def s(self, fn): self.op("scalar", fn)
    def g(self, fn): self.op("gpsimd", fn)

    SPLIT = ("vector", "scalar", "gpsimd")

    def emit(self, blk, sem):
        runs = []
        for (eng, fn, is_dma) in self.ops:
            if runs and runs[-1][0] == eng and eng not in self.SPLIT:
                runs[-1][1].append((fn, is_dma))
            else:
                runs.append((eng, [(fn, is_dma)]))
        waits, acc = [], 0
        for (eng, items) in runs:
            waits.append(acc)
            ndma = sum(1 for (_, d) in items if d)
            acc += 16 * ndma + (1 if len(items) > ndma else 0)
        self.total = acc
        per_engine = {}
        for i, (eng, items) in enumerate(runs):
            per_engine.setdefault(eng, []).append((waits[i], items))
        for eng_name, eruns in per_engine.items():
            def body(e, eruns=eruns):
                for (w, items) in eruns:
                    if w > 0:
                        e.wait_ge(sem, w)
                    last_nd = max((i for i, (_, d) in enumerate(items) if not d),
                                  default=-1)
                    for i, (fn, d) in enumerate(items):
                        ins = fn(e)
                        if d:
                            ins.then_inc(sem, 16)
                        elif i == last_nd:
                            ins.then_inc(sem, 1)
            getattr(blk, eng_name)(body)

        def fin(e):
            e.wait_ge(sem, acc)
        blk.sync(fin)


class SB:
    def __init__(self, nc, base=16512, limit=229344):
        self.nc = nc
        self.off = base
        self.limit = limit
        self.marks = []
        self.n = 0

    def alloc(self, name, shape, dtype, align=64):
        nbytes = int(np.prod(shape[1:])) * mybir.dt.size(dtype)
        self.off = (self.off + align - 1) // align * align
        self.n += 1
        h = self.nc.alloc_sbuf_tensor_at(f"{name}_{self.n}", list(shape), dtype,
                                         offset=self.off)
        self.off += nbytes
        assert self.off <= self.limit, f"SBUF overflow at {name}: {self.off}"
        return h.ap()

    def mark(self):
        self.marks.append(self.off)

    def reset(self):
        self.off = self.marks.pop()


def build_program(debug=False):
    nc = bass.Bass("TRN2", target_bir_lowering=False)
    p = Prog()

    x_in = nc.dram_tensor("x", [T, DM], F32, kind="ExternalInput").ap()
    W = {}
    wshapes = {
        "L0_attn_a_w_qkv": [3 * DM, DM], "L0_attn_a_b_qkv": [3 * DM],
        "L0_attn_b_ln1_g": [DM], "L0_attn_b_ln1_b": [DM],
        "L0_attn_b_in_proj_w": [2 * DI, DM], "L0_attn_b_conv_w": [DI, 4],
        "L0_attn_b_conv_b": [DI], "L0_attn_b_x_proj_w": [96, DI],
        "L0_attn_b_dt_proj_w": [DI, DTR], "L0_attn_b_dt_proj_b": [DI],
        "L0_attn_b_A_log": [DI, NST], "L0_attn_b_D": [DI],
        "L0_attn_b_out_proj_w": [DM, DI], "L0_attn_b_ln2_g": [DM],
        "L0_attn_b_ln2_b": [DM], "L0_attn_b_mlp_w1": [2 * DM, DM],
        "L0_attn_b_mlp_b1": [2 * DM], "L0_attn_b_mlp_w2": [DM, 2 * DM],
        "L0_attn_b_mlp_b2": [DM], "L0_ffn_w_gate": [FI, DM],
        "L0_ffn_w_up": [FI, DM], "L0_ffn_w_down": [DM, FI],
    }
    for k, shp in wshapes.items():
        W[k] = nc.dram_tensor(k, shp, F32, kind="ExternalInput").ap()
    out_d = nc.dram_tensor("out", [T, DM], F32, kind="ExternalOutput").ap()

    x2s = nc.dram_tensor("x2s", [T, DM], F32, kind="Internal").ap()
    x3s = nc.dram_tensor("x3s", [T, DM], F32, kind="Internal").ap()
    wqkT = nc.dram_tensor("wqkT", [DM, 2048], BF16, kind="Internal").ap()
    wvT = nc.dram_tensor("wvT", [DM, DM], BF16, kind="Internal").ap()
    wguT = nc.dram_tensor("wguT", [DM, 2 * FI], BF16, kind="Internal").ap()
    wdT = nc.dram_tensor("wdT", [FI, DM], BF16, kind="Internal").ap()
    winT = nc.dram_tensor("winT", [DM, 2 * DI], BF16, kind="Internal").ap()
    woT = nc.dram_tensor("woT", [DI, DM], BF16, kind="Internal").ap()
    w1T = nc.dram_tensor("w1T", [DM, 2 * DM], BF16, kind="Internal").ap()
    w2T = nc.dram_tensor("w2T", [2 * DM, DM], BF16, kind="Internal").ap()
    xpT = nc.dram_tensor("xpT", [DI, 96], BF16, kind="Internal").ap()
    bcs = nc.dram_tensor("bcs", [32, 512], F32, kind="Internal").ap()
    dtT = nc.dram_tensor("dtT", [DTR, DI], BF16, kind="Internal").ap()

    sb = SB(nc)
    ident = sb.alloc("ident", [128, 128], BF16)
    ones_bf = sb.alloc("ones_bf", [128, 1], BF16)
    bqk_f = sb.alloc("bqk_f", [128, 16], F32)
    bv_bc = sb.alloc("bv_bc", [128, DM], F32)
    convw = sb.alloc("convw", [128, 64], F32)
    convb = sb.alloc("convb", [128, 16], F32)
    dtbn = sb.alloc("dtbn", [128, 16], F32)
    A_pos = sb.alloc("A_pos", [128, 256], F32)
    D_f = sb.alloc("D_f", [128, 16], F32)
    bias1 = sb.alloc("bias1", [128, 16], F32)
    c0in = sb.alloc("c0in", [128, 32], F32)
    b2_bc = sb.alloc("b2_bc", [128, DM], F32)
    g1_f = sb.alloc("g1_f", [128, 8], F32)
    g2_f = sb.alloc("g2_f", [128, 8], F32)
    b1f_bf = sb.alloc("b1f_bf", [128, 8], BF16)
    b2f_bf = sb.alloc("b2f_bf", [128, 8], BF16)

    PS = [nc.alloc_psum_tensor(f"ps{i}", [128, 512], F32).ap() for i in range(8)]

    def psb(i):
        return PS[i].bitcast(BF16)

    def stream_ap(hbm_ap, tok0, nj):
        return bass.AP(hbm_ap.tensor, tok0 * DM,
                       [[DM, 128], [128 * DM, nj], [1, DM]])

    def emit_ln_stats(xap, ncols, m_ap, r_ap, sq_scratch):
        p.v(lambda v: v.reduce_sum(m_ap, xap, axis=AX))
        p.s(lambda s: s.activation(sq_scratch[:, 0:ncols], xap, AF.Square,
                                   accum_out=r_ap))
        p.v(lambda v: v.tensor_scalar_mul(m_ap, m_ap, 1.0 / ncols))
        p.v(lambda v: v.scalar_tensor_tensor(sq_scratch[:, 0:1], m_ap, 1.0,
                                             m_ap, ALU.mult, ALU.mult))
        p.v(lambda v: v.tensor_scalar(r_ap, r_ap, 1.0 / ncols, 1e-5,
                                      ALU.mult, ALU.add))
        p.v(lambda v: v.tensor_sub(r_ap, r_ap, sq_scratch[:, 0:1]))
        p.s(lambda s: s.activation(r_ap, r_ap, AF.Sqrt))
        p.v(lambda v: v.reciprocal(r_ap, r_ap))

    def emit_quant(src, dst_bf, ncols, bmax, sc2, s1, s2):
        nb = ncols // 16
        src3 = src.rearrange("p (b k) -> p b k", k=16)
        p.v(lambda v: v.reduce_max(bmax[:, 0:nb], src3, axis=AX,
                                   apply_absolute_value=True))
        p.v(lambda v: v.tensor_scalar(sc2[:, 0:nb], bmax[:, 0:nb], 1.0 / 3.0,
                                      2e-12, ALU.mult, ALU.max))
        p.v(lambda v: v.tensor_scalar(bmax[:, 0:nb], bmax[:, 0:nb], 1.0 / 6.0,
                                      1e-12, ALU.mult, ALU.max))
        p.v(lambda v: v.reciprocal(bmax[:, 0:nb], bmax[:, 0:nb]))
        rb = bmax[:, 0:nb].unsqueeze(2).broadcast_to([128, nb, 16])
        s2b = sc2[:, 0:nb].unsqueeze(2).broadcast_to([128, nb, 16])
        xn = s1[:, 0:ncols]
        xn3 = xn.rearrange("p (b k) -> p b k", k=16)
        p.v(lambda v: v.tensor_tensor(xn3, src3, rb, ALU.mult))
        y2 = src[:, 0:ncols]
        p.s(lambda s_: s_.activation(y2, xn, AF.Abs, scale=f32(2.0)))
        t_lo = s2[:, 0:ncols]
        t_mid = src[:, 0:ncols]
        tm = PS[7][:, 0:min(ncols, 512)]
        p.v(lambda v: v.tensor_scalar(xn, xn, 0.0, 0.5, ALU.is_ge, ALU.subtract))
        p.v(lambda v: v.tensor_scalar(t_lo, y2, 4.0, MAGIC, ALU.min, ALU.add))
        p.v(lambda v: v.tensor_scalar(t_lo, t_lo, MAGIC, 0.5, ALU.subtract,
                                      ALU.mult))
        nseg = (ncols + 511) // 512
        for sgi in range(nseg):
            a, b = sgi * 512, min((sgi + 1) * 512, ncols)
            w_ = b - a
            p.v(lambda v, a=a, b=b, w_=w_: v.tensor_scalar(
                tm[:, 0:w_], y2[:, a:b], 8.0, 0.5, ALU.min, ALU.mult))
            p.v(lambda v, w_=w_: v.tensor_scalar(
                tm[:, 0:w_], tm[:, 0:w_], MAGIC, MAGIC + 2.0, ALU.add,
                ALU.subtract))
            p.v(lambda v, a=a, b=b, w_=w_: v.scalar_tensor_tensor(
                t_lo[:, a:b], tm[:, 0:w_], 0.0, t_lo[:, a:b], ALU.max, ALU.add))
            p.v(lambda v, a=a, b=b, w_=w_: v.tensor_scalar(
                tm[:, 0:w_], y2[:, a:b], 0.25, MAGIC, ALU.mult, ALU.add))
            p.v(lambda v, w_=w_: v.tensor_scalar(
                tm[:, 0:w_], tm[:, 0:w_], MAGIC + 2.0, 2.0, ALU.subtract,
                ALU.mult))
            p.v(lambda v, a=a, b=b, w_=w_: v.scalar_tensor_tensor(
                t_lo[:, a:b], tm[:, 0:w_], 0.0, t_lo[:, a:b], ALU.max, ALU.add))
        p.v(lambda v: v.tensor_mul(t_lo, t_lo, xn))
        t3 = t_lo.rearrange("p (b k) -> p b k", k=16)
        d3 = dst_bf.rearrange("p (b k) -> p b k", k=16)
        p.v(lambda v: v.tensor_tensor(d3, t3, s2b, ALU.mult))

    p.v(lambda v: v.memset(ones_bf, 1.0))
    p.g(lambda g: g.affine_select(ident, ones_bf.broadcast_to([128, 128]),
                                  [[-1, 128]], ALU.is_equal, 0.0, base=0,
                                  channel_multiplier=1))
    p.dma("sync", bqk_f, W["L0_attn_a_b_qkv"].rearrange("(c p) -> p c", p=128)[:, 0:16], allow_slow_non_contiguous=True)
    p.dma("sync", bv_bc, W["L0_attn_a_b_qkv"][None, 2048:3072].partition_broadcast(128)[:, 0, :])
    p.dma("sync", convw, bass.AP(W["L0_attn_b_conv_w"].tensor, 0,
                                 [[4, 128], [512, 16], [1, 4]]))
    p.dma("sync", convb, W["L0_attn_b_conv_b"].rearrange("(c p) -> p c", p=128), allow_slow_non_contiguous=True)
    p.dma("sync", D_f, W["L0_attn_b_D"].rearrange("(c p) -> p c", p=128), allow_slow_non_contiguous=True)
    p.dma("sync", b2_bc, W["L0_attn_b_mlp_b2"][None, :].partition_broadcast(128)[:, 0, :])
    p.dma("sync", g1_f, W["L0_attn_b_ln1_g"].rearrange("(c p) -> p c", p=128), allow_slow_non_contiguous=True)
    p.dma("sync", g2_f, W["L0_attn_b_ln2_g"].rearrange("(c p) -> p c", p=128), allow_slow_non_contiguous=True)
    p.dma("sync", dtbn, W["L0_attn_b_dt_proj_b"].rearrange("(c p) -> p c", p=128), allow_slow_non_contiguous=True)
    p.v(lambda v: v.tensor_scalar_mul(dtbn, dtbn, -1.0))

    sb.mark()
    tmpb = sb.alloc("prep_tmp", [128, FI], F32)
    tbf = sb.alloc("prep_bf", [128, FI], BF16)
    tbf2 = sb.alloc("prep_bf2", [128, FI], BF16)
    qsc = sb.alloc("prep_qsc", [128, FI // 16], F32)
    qsc2 = sb.alloc("prep_qsc2", [128, FI // 16], F32)
    qs1 = sb.alloc("prep_qs1", [128, FI], F32)
    qs2 = sb.alloc("prep_qs2", [128, FI], F32)
    mlpb1f = sb.alloc("prep_mlpb1", [128, 16], F32)

    p.dma("sync", tmpb[:, 0:256], bass.AP(W["L0_attn_b_A_log"].tensor, 0,
                                           [[16, 128], [2048, 16], [1, 16]]))
    p.s(lambda s: s.activation(A_pos, tmpb[:, 0:256], AF.Exp))
    p.dma("sync", mlpb1f, W["L0_attn_b_mlp_b1"].rearrange("(c p) -> p c", p=128), allow_slow_non_contiguous=True)
    p.dma("sync", tmpb[:, 0:8], W["L0_attn_b_ln1_b"].rearrange("(c p) -> p c", p=128), allow_slow_non_contiguous=True)
    p.v(lambda v: v.tensor_copy(b1f_bf, tmpb[:, 0:8]))
    p.dma("sync", tmpb[:, 8:16], W["L0_attn_b_ln2_b"].rearrange("(c p) -> p c", p=128), allow_slow_non_contiguous=True)
    p.v(lambda v: v.tensor_copy(b2f_bf, tmpb[:, 8:16]))

    def prep_w(src_hbm, rows, cols, dst_hbm, quant=False, gfold=None,
               c0_rhs=None, c0_ps=None):
        rt, ct = rows // 128, cols // 128
        for r in range(rt):
            p.dma("sync", tmpb[:, 0:cols], src_hbm[r * 128:(r + 1) * 128, :])
            if quant:
                emit_quant(tmpb[:, 0:cols], tbf[:, 0:cols], cols, qsc, qsc2,
                           qs1, qs2)
            else:
                p.v(lambda v: v.tensor_copy(tbf[:, 0:cols], tmpb[:, 0:cols]))
            for c_ in range(ct):
                p.dmat("sync", tbf2[:, c_ * 128:(c_ + 1) * 128],
                       tbf[:, c_ * 128:(c_ + 1) * 128])
            if c0_rhs is not None:
                for c_ in range(ct):
                    p.t(lambda t, c_=c_, r=r: t.matmul(
                        c0_ps[:, r:r + 1],
                        tbf2[:, c_ * 128:(c_ + 1) * 128],
                        c0_rhs[:, c_:c_ + 1],
                        start=(c_ == 0), stop=(c_ == ct - 1)))
            if gfold is not None:
                for c_ in range(ct):
                    p.v(lambda v, c_=c_: v.tensor_scalar(
                        tbf2[:, c_ * 128:(c_ + 1) * 128],
                        tbf2[:, c_ * 128:(c_ + 1) * 128],
                        gfold[:, c_:c_ + 1], None, ALU.mult))
            for c_ in range(ct):
                p.dma("sync", dst_hbm[c_ * 128:(c_ + 1) * 128,
                                      r * 128:(r + 1) * 128],
                      tbf2[:, c_ * 128:(c_ + 1) * 128])

    prep_w(W["L0_attn_a_w_qkv"][0:2048, :], 2048, DM, wqkT)
    prep_w(W["L0_attn_a_w_qkv"][2048:3072, :], 1024, DM, wvT)
    prep_w(W["L0_ffn_w_gate"], FI, DM, wguT[:, 0:FI], quant=True)
    prep_w(W["L0_ffn_w_up"], FI, DM, wguT[:, FI:2 * FI], quant=True)
    prep_w(W["L0_ffn_w_down"], DM, FI, wdT, quant=True)
    prep_w(W["L0_attn_b_in_proj_w"], 2 * DI, DM, winT, gfold=g1_f,
           c0_rhs=b1f_bf, c0_ps=PS[6])
    p.v(lambda v: v.tensor_copy(c0in, PS[6][:, 0:32]))
    prep_w(W["L0_attn_b_mlp_w1"], 2 * DM, DM, w1T, gfold=g2_f,
           c0_rhs=b2f_bf, c0_ps=PS[5])
    p.v(lambda v: v.tensor_add(bias1, PS[5][:, 0:16], mlpb1f))
    prep_w(W["L0_attn_b_out_proj_w"], DM, DI, woT)
    prep_w(W["L0_attn_b_mlp_w2"], DM, 2 * DM, w2T)
    for r in range(16):
        p.dma("sync", tmpb[:, 0:64], W["L0_attn_b_dt_proj_w"][r * 128:(r + 1) * 128, :])
        p.v(lambda v: v.tensor_copy(tbf[:, 0:64], tmpb[:, 0:64]))
        p.t(lambda t: t.transpose(psb(4)[0:64, 0:128], tbf[:, 0:64], ident))
        p.s(lambda s: s.copy(tbf2[0:64, 0:128], psb(4)[0:64, 0:128]))
        p.dma("sync", dtT[:, r * 128:(r + 1) * 128], tbf2[0:64, 0:128])
    for c_ in range(16):
        p.dma("sync", tmpb[0:96, 0:128],
              W["L0_attn_b_x_proj_w"][:, c_ * 128:(c_ + 1) * 128])
        p.v(lambda v: v.tensor_copy(tbf[0:96, 0:128], tmpb[0:96, 0:128]))
        p.t(lambda t: t.transpose(psb(4)[:, 0:96], tbf[0:96, 0:128], ident[0:96, 0:96]))
        p.s(lambda s: s.copy(tbf2[:, 0:96], psb(4)[:, 0:96]))
        p.dma("sync", xpT[c_ * 128:(c_ + 1) * 128, :], tbf2[:, 0:96])
    sb.reset()

    dbg = {}
    if debug:
        for nm, shp, dt in [("d_attn", [128, 32 * 1024], BF16),
                            ("d_x2s", [T, DM], F32), ("d_x3s", [T, DM], F32),
                            ("d_wqkT", [DM, 2048], BF16),
                            ("d_wguT", [DM, 2 * FI], BF16),
                            ("d_winT", [DM, 2 * DI], BF16),
                            ("d_Apos", [128, 256], F32),
                            ("d_c0in", [128, 32], F32)]:
            dbg[nm] = nc.dram_tensor(nm, shp, dt, kind="ExternalOutput").ap()
        for nm, shp, dt in [("d_xh", [128, 4 * DM], BF16),
                            ("d_qkf", [128, 16 * 512], BF16),
                            ("d_vtm", [128, 4 * DM], BF16),
                            ("d_sc", [128, 128], F32),
                            ("d_P", [128, 128], BF16),
                            ("d_ao", [128, 64], BF16)]:
            dbg[nm] = nc.dram_tensor(nm, shp, dt, kind="ExternalOutput").ap()
        p.dma("sync", dbg["d_wqkT"], wqkT)
        p.dma("sync", dbg["d_wguT"], wguT)
        p.dma("sync", dbg["d_winT"], winT)
        p.dma("sync", dbg["d_Apos"], A_pos)
        p.dma("sync", dbg["d_c0in"], c0in)

    sb.mark()
    attn_scr = sb.alloc("attn_scr", [128, 32 * 1024], BF16)

    sb.mark()
    x0t = sb.alloc("a_x0t", [128, 4 * DM], F32)
    xh = sb.alloc("a_xh", [128, 4 * DM], BF16)
    xhT = sb.alloc("a_xhT", [128, 8 * 512], BF16)
    qkf = sb.alloc("a_qkf", [128, 16 * 512], BF16)
    vtm = sb.alloc("a_v", [128, 4 * DM], BF16)
    wqk_s = sb.alloc("a_wqk", [128, 8 * 2048], BF16)
    wv_s = sb.alloc("a_wv", [128, 8 * 1024], BF16)
    Pt = sb.alloc("a_P", [128, 128], BF16)
    PTt = sb.alloc("a_PT", [128, 128], BF16)
    ao = sb.alloc("a_ao", [128, 64], BF16)
    astat = sb.alloc("a_stat", [128, 16], F32)
    amx = sb.alloc("a_mx", [128, 4], F32)
    asum = sb.alloc("a_sume", [128, 2], F32)
    asq = sb.alloc("a_sq", [128, DM], F32)

    for kk in range(8):
        p.dma("sync", wqk_s[:, kk * 2048:(kk + 1) * 2048],
              wqkT[kk * 128:(kk + 1) * 128, :])
        p.dma("sync", wv_s[:, kk * 1024:(kk + 1) * 1024],
              wvT[kk * 128:(kk + 1) * 128, :])

    for c in range(8):
        p.dma("sync", x0t, stream_ap(x_in, c * 512, 4))
        for j in range(4):
            xs = x0t[:, j * DM:(j + 1) * DM]
            emit_ln_stats(xs, DM, astat[:, j:j + 1], astat[:, 4 + j:5 + j], asq)
            p.v(lambda v, j=j, xs=xs: v.tensor_scalar(
                xh[:, j * DM:(j + 1) * DM], xs, astat[:, j:j + 1],
                astat[:, 4 + j:5 + j], ALU.subtract, ALU.mult))
        for kk in range(8):
            for j in range(4):
                p.dmat("sync",
                       xhT[:, kk * 512 + j * 128: kk * 512 + (j + 1) * 128],
                       xh[:, j * DM + kk * 128: j * DM + (kk + 1) * 128])
        for do in range(16):
            bank = do % 4
            for kk in range(8):
                p.t(lambda t, do=do, kk=kk, bank=bank: t.matmul(
                    PS[bank],
                    wqk_s[:, kk * 2048 + do * 128: kk * 2048 + (do + 1) * 128],
                    xhT[:, kk * 512:(kk + 1) * 512],
                    start=(kk == 0), stop=(kk == 7)))
            p.s(lambda s, do=do, bank=bank: s.activation(
                qkf[:, do * 512:(do + 1) * 512], PS[bank], AF.Identity,
                bias=bqk_f[:, do:do + 1]))
        for j in range(4):
            for dv in range(2):
                bank = 4 + dv
                for kk in range(8):
                    p.t(lambda t, j=j, dv=dv, kk=kk, bank=bank: t.matmul(
                        PS[bank],
                        xhT[:, kk * 512 + j * 128: kk * 512 + (j + 1) * 128],
                        wv_s[:, kk * 1024 + dv * 512: kk * 1024 + (dv + 1) * 512],
                        start=(kk == 0), stop=(kk == 7)))
                p.v(lambda v, j=j, dv=dv, bank=bank: v.scalar_tensor_tensor(
                    vtm[:, j * DM + dv * 512: j * DM + (dv + 1) * 512],
                    PS[bank], 1.0, bv_bc[:, dv * 512:(dv + 1) * 512],
                    ALU.mult, ALU.add))
        if debug and c == 0:
            p.dma("sync", dbg["d_xh"], xh)
            p.dma("sync", dbg["d_qkf"], qkf)
            p.dma("sync", dbg["d_vtm"], vtm)
        for j in range(4):
            beta = c * 4 + j
            tg, hf = beta // 2, beta % 2
            for h in range(NH):
                qsl = qkf[(h % 2) * 64:(h % 2) * 64 + 64,
                          (h // 2) * 512 + j * 128:(h // 2) * 512 + (j + 1) * 128]
                ksl = qkf[(h % 2) * 64:(h % 2) * 64 + 64,
                          (8 + h // 2) * 512 + j * 128:(8 + h // 2) * 512 + (j + 1) * 128]
                p.t(lambda t, qsl=qsl, ksl=ksl: t.matmul(
                    PS[6][:, 0:128], qsl, ksl, start=True, stop=True))
                if debug and c == 0 and j == 0 and h == 0:
                    p.s(lambda s_: s_.copy(asq[:, 0:128], PS[6][:, 0:128]))
                    p.dma("sync", dbg["d_sc"], asq[:, 0:128])
                p.v(lambda v: v.reduce_max(amx[:, 0:1], PS[6][:, 0:128], axis=AX))
                p.v(lambda v: v.tensor_scalar_mul(amx[:, 1:2], amx[:, 0:1],
                                                  -1.0 / 8.0))
                p.s(lambda s: s.activation(Pt, PS[6][:, 0:128], AF.Exp,
                                           bias=amx[:, 1:2],
                                           scale=f32(1.0 / 8.0),
                                           accum_out=asum[:, 0:1]))
                p.t(lambda t: t.transpose(psb(7)[:, 0:128], Pt, ident))
                p.s(lambda s: s.copy(PTt, psb(7)[:, 0:128]))
                p.v(lambda v: v.reciprocal(asum[:, 1:2], asum[:, 0:1]))
                vsl = vtm[:, j * DM + h * 64: j * DM + h * 64 + 64]
                p.t(lambda t, vsl=vsl: t.matmul(PS[6][:, 0:64], PTt, vsl,
                                                start=True, stop=True))
                p.s(lambda s: s.activation(ao, PS[6][:, 0:64], AF.Identity,
                                           scale=asum[:, 1:2]))
                if debug and c == 0 and j == 0 and h == 0:
                    p.dma("sync", dbg["d_P"], Pt)
                    p.dma("sync", dbg["d_ao"], ao)
                t0 = 256 * h + 16 * tg + 8 * hf
                p.dma("sync",
                      attn_scr[t0 % 128:t0 % 128 + 8,
                               (t0 // 128) * 1024:(t0 // 128 + 1) * 1024],
                      ao)
    sb.reset()

    def emit_ffn_phase(src_hbm, dst_hbm, add_attn):
        sb.mark()
        x1t = sb.alloc("f_x1t", [128, 4 * DM], F32)
        xqh = sb.alloc("f_xh", [128, DM], F32)
        xq = sb.alloc("f_xq", [128, 4 * DM], BF16)
        xqT = sb.alloc("f_xqT", [128, 8 * 512], BF16)
        sg = sb.alloc("f_sg", [128, FI], BF16)
        hh = sb.alloc("f_h", [128, FI], F32)
        hq = sb.alloc("f_hq", [128, FI], BF16)
        hqT = sb.alloc("f_hqT", [128, 22 * 128], BF16)
        x2t = sb.alloc("f_x2t", [128, 4 * DM], F32)
        fstat = sb.alloc("f_stat", [128, 16], F32)
        fsq = sb.alloc("f_sq", [128, DM], F32)
        bmax = sb.alloc("f_bmax", [128, FI // 16], F32)
        sc2 = sb.alloc("f_sc2", [128, FI // 16], F32)
        fq1 = sb.alloc("f_q1", [128, FI], F32)
        fq2 = sb.alloc("f_q2", [128, FI], F32)
        wk = sb.alloc("f_wk", [128, 512], BF16)
        wd_s = sb.alloc("f_wd", [128, DM], BF16)

        for c in range(8):
            p.dma("sync", x1t, stream_ap(src_hbm, c * 512, 4))
            if add_attn:
                for j in range(4):
                    p.v(lambda v, c=c, j=j: v.scalar_tensor_tensor(
                        x1t[:, j * DM:(j + 1) * DM],
                        attn_scr[:, (c * 4 + j) * 1024:(c * 4 + j + 1) * 1024],
                        1.0, x1t[:, j * DM:(j + 1) * DM], ALU.mult, ALU.add))
            for j in range(4):
                xs = x1t[:, j * DM:(j + 1) * DM]
                emit_ln_stats(xs, DM, fstat[:, j:j + 1], fstat[:, 4 + j:5 + j],
                              fsq)
                p.v(lambda v, j=j, xs=xs: v.tensor_scalar(
                    xqh, xs, fstat[:, j:j + 1], fstat[:, 4 + j:5 + j],
                    ALU.subtract, ALU.mult))
                emit_quant(xqh, xq[:, j * DM:(j + 1) * DM], DM, bmax, sc2,
                           fq1, fq2)
            for kk in range(8):
                for j in range(4):
                    p.dmat("sync",
                           xqT[:, kk * 512 + j * 128: kk * 512 + (j + 1) * 128],
                           xq[:, j * DM + kk * 128: j * DM + (kk + 1) * 128])
            for j in range(4):
                for grp, ta, tb_ in ((0, 0, 6), (1, 6, 11)):
                    for dt_ in range(ta, tb_):
                        bank = dt_ - ta
                        for kk in range(8):
                            p.dma("sync", wk,
                                  wguT[kk * 128:(kk + 1) * 128,
                                       dt_ * 512:(dt_ + 1) * 512])
                            p.t(lambda t, j=j, kk=kk, bank=bank: t.matmul(
                                PS[bank],
                                xqT[:, kk * 512 + j * 128: kk * 512 + (j + 1) * 128],
                                wk, start=(kk == 0), stop=(kk == 7)))
                    for dt_ in range(ta, tb_):
                        bank = dt_ - ta
                        if dt_ < 5:
                            p.s(lambda s, dt_=dt_, bank=bank: s.activation(
                                sg[:, dt_ * 512:(dt_ + 1) * 512], PS[bank],
                                AF.Silu))
                        elif dt_ == 5:
                            p.s(lambda s, bank=bank: s.activation(
                                sg[:, 2560:2816], PS[bank][:, 0:256], AF.Silu))
                            p.v(lambda v, bank=bank: v.tensor_tensor(
                                hh[:, 0:256], PS[bank][:, 256:512],
                                sg[:, 0:256], ALU.mult))
                        else:
                            u0 = dt_ * 512 - FI
                            p.v(lambda v, bank=bank, u0=u0: v.tensor_tensor(
                                hh[:, u0:u0 + 512], PS[bank],
                                sg[:, u0:u0 + 512], ALU.mult))
                emit_quant(hh, hq, FI, bmax, sc2, fq1, fq2)
                for kk in range(22):
                    p.dmat("sync", hqT[:, kk * 128:(kk + 1) * 128],
                           hq[:, kk * 128:(kk + 1) * 128])
                for kk in range(22):
                    p.dma("sync", wd_s, wdT[kk * 128:(kk + 1) * 128, :])
                    for dv in range(2):
                        p.t(lambda t, kk=kk, dv=dv: t.matmul(
                            PS[6 + dv], hqT[:, kk * 128:(kk + 1) * 128],
                            wd_s[:, dv * 512:(dv + 1) * 512],
                            start=(kk == 0), stop=(kk == 21)))
                for dv in range(2):
                    p.v(lambda v, j=j, dv=dv: v.scalar_tensor_tensor(
                        x2t[:, j * DM + dv * 512: j * DM + (dv + 1) * 512],
                        PS[6 + dv], 1.0,
                        x1t[:, j * DM + dv * 512: j * DM + (dv + 1) * 512],
                        ALU.mult, ALU.add))
            p.dma("sync", stream_ap(dst_hbm, c * 512, 4), x2t)
        sb.reset()

    if debug:
        p.dma("sync", dbg["d_attn"], attn_scr)
    emit_ffn_phase(x_in, x2s, add_attn=True)
    if debug:
        p.dma("sync", dbg["d_x2s"], x2s)
    sb.reset()

    sb.mark()
    TC = 512
    m_x2 = sb.alloc("m_x2", [128, 4 * DM], F32)
    m_xi1 = sb.alloc("m_xi1", [128, 4 * DM], F32)
    m_R1 = sb.alloc("m_R1", [128, 16 * TC], BF16)
    m_R2 = sb.alloc("m_R2", [128, 16 * TC], BF16)
    m_z = sb.alloc("m_z", [128, 16 * TC], BF16)
    m_uu = sb.alloc("m_uu", [128, 16 * TC], BF16)
    m_wk = sb.alloc("m_wk", [128, 1024], BF16)
    m_xdbl = sb.alloc("m_xdbl", [128, TC], F32)
    m_stat = sb.alloc("m_stat", [128, 32], F32)
    m_sq = sb.alloc("m_sq", [128, DM], F32)
    m_state = sb.alloc("m_state", [128, 256], F32)
    m_tail = sb.alloc("m_tail", [128, 64], BF16)
    m_xc = sb.alloc("m_xc", [128, TC], F32)

    m_u = m_uu[:, 0:4 * DM]
    m_uT = m_uu[:, 4 * DM:4 * DM + 8 * TC]
    m_scr = m_uu.bitcast(F32)
    m_xm = m_R1
    m_xa = m_R1
    m_y3 = m_R2
    m_gl = m_R2
    m_x3 = m_R1.bitcast(F32)
    m_delta = m_scr[:, 0 * TC:1 * TC]
    m_du = m_scr[:, 1 * TC:2 * TC]
    m_dA = m_scr[:, 2 * TC:3 * TC]
    m_dBu = m_scr[:, 3 * TC:4 * TC]
    m_h = m_scr[:, 4 * TC:5 * TC]
    m_y = m_scr[:, 5 * TC:6 * TC]
    m_Br = m_scr[:, 6 * TC:7 * TC]
    m_Crep = m_scr[:, 7 * TC:8 * TC]
    m_xcb = m_xc.bitcast(BF16)
    m_sz = m_xcb[:, 0:TC]
    m_dtb = m_xcb[:, TC:2 * TC]

    p.v(lambda v: v.memset(m_state, 0.0))
    p.v(lambda v: v.memset(m_tail, 0.0))

    NCH = T // TC
    for c in range(NCH):
        p.dma("sync", m_x2, stream_ap(x2s, c * TC, 4))
        for j in range(4):
            xs = m_x2[:, j * DM:(j + 1) * DM]
            emit_ln_stats(xs, DM, m_stat[:, j:j + 1], m_stat[:, 8 + j:9 + j],
                          m_sq)
            r1 = m_stat[:, 8 + j:9 + j]
            t1 = m_stat[:, 16 + j:17 + j]
            p.v(lambda v, r1=r1, t1=t1: v.tensor_tensor(t1, r1, r1, ALU.mult))
            p.v(lambda v, t1=t1: v.tensor_scalar(t1, t1, -1e-5, 1.0 + 1e-5,
                                                 ALU.mult, ALU.add))
            p.s(lambda s, t1=t1: s.activation(t1, t1, AF.Sqrt))
            p.v(lambda v, t1=t1: v.reciprocal(t1, t1))
            p.v(lambda v, r1=r1, t1=t1: v.tensor_tensor(t1, t1, r1, ALU.mult))
            p.v(lambda v, j=j, xs=xs, t1=t1: v.tensor_scalar(
                m_u[:, j * DM:(j + 1) * DM], xs, m_stat[:, j:j + 1], t1,
                ALU.subtract, ALU.mult))
        for kk in range(8):
            for j in range(4):
                p.dmat("sync",
                       m_uT[:, kk * TC + j * 128: kk * TC + (j + 1) * 128],
                       m_u[:, j * DM + kk * 128: j * DM + (kk + 1) * 128])
        for do in range(32):
            p.dma("sync", m_wk[:, 0:1024],
                  bass.AP(winT.tensor, do * 128,
                          [[4096, 128], [524288, 8], [1, 128]]))
            for tb in range(1):
                bank = (do * 2 + tb) % 4
                for kk in range(8):
                    p.t(lambda t, kk=kk, tb=tb, bank=bank: t.matmul(
                        PS[bank], m_wk[:, kk * 128:(kk + 1) * 128],
                        m_uT[:, kk * TC + tb * 512: kk * TC + (tb + 1) * 512],
                        start=(kk == 0), stop=(kk == 7)))
                dst = m_xm if do < 16 else m_z
                dcol = (do % 16) * TC + tb * 512
                p.s(lambda s, dst=dst, dcol=dcol, bank=bank, do=do: s.activation(
                    dst[:, dcol:dcol + 512], PS[bank], AF.Identity,
                    bias=c0in[:, do:do + 1]))
        for dt_ in range(16):
            base = dt_ * TC
            p.v(lambda v, dt_=dt_, base=base: v.tensor_scalar(
                m_xc, m_xm[:, base:base + TC],
                convw[:, dt_ * 4 + 3:dt_ * 4 + 4],
                convb[:, dt_:dt_ + 1], ALU.mult, ALU.add))
            for k in range(3):
                sh = 3 - k
                p.v(lambda v, base=base, k=k, sh=sh, dt_=dt_:
                    v.scalar_tensor_tensor(
                        m_xc[:, sh:TC], m_xm[:, base:base + TC - sh],
                        convw[:, dt_ * 4 + k:dt_ * 4 + k + 1], m_xc[:, sh:TC],
                        ALU.mult, ALU.add))
                for i in range(sh):
                    p.v(lambda v, dt_=dt_, k=k, i=i: v.scalar_tensor_tensor(
                        m_xc[:, i:i + 1],
                        m_tail[:, dt_ * 4 + k + i: dt_ * 4 + k + i + 1],
                        convw[:, dt_ * 4 + k:dt_ * 4 + k + 1],
                        m_xc[:, i:i + 1], ALU.mult, ALU.add))
            p.v(lambda v, dt_=dt_, base=base: v.tensor_copy(
                m_tail[:, dt_ * 4:dt_ * 4 + 3],
                m_xm[:, base + TC - 3:base + TC]))
            p.s(lambda s, base=base: s.activation(
                m_xa[:, base:base + TC], m_xc, AF.Silu))
        for kk in range(16):
            p.dma("sync", m_wk[:, 0:96], xpT[kk * 128:(kk + 1) * 128, :])
            for tb in range(1):
                p.t(lambda t, kk=kk, tb=tb: t.matmul(
                    PS[4 + tb][0:96, :], m_wk[:, 0:96],
                    m_xa[:, kk * TC + tb * 512: kk * TC + (tb + 1) * 512],
                    start=(kk == 0), stop=(kk == 15)))
        for tb in range(1):
            p.v(lambda v, tb=tb: v.tensor_copy(
                m_xdbl[0:96, tb * 512:(tb + 1) * 512], PS[4 + tb][0:96, :]))
        p.v(lambda v: v.tensor_copy(m_dtb[0:64, :], m_xdbl[0:64, :]))
        p.dma("sync", bcs[0:32, :], m_xdbl[64:96, :])
        for dt_ in range(16):
            base = dt_ * TC
            p.dma("sync", m_wk[0:64, 0:128], dtT[:, dt_ * 128:(dt_ + 1) * 128])
            for tb in range(1):
                p.t(lambda t, tb=tb: t.matmul(
                    PS[6 + tb], m_wk[0:64, 0:128],
                    m_dtb[0:64, tb * 512:(tb + 1) * 512],
                    start=True, stop=True))
            for tb in range(1):
                p.s(lambda s, tb=tb, dt_=dt_: s.activation(
                    m_delta[:, tb * 512:(tb + 1) * 512], PS[6 + tb],
                    AF.Sigmoid, scale=f32(-1.0), bias=dtbn[:, dt_:dt_ + 1]))
            p.s(lambda s: s.activation(m_delta, m_delta, AF.Ln))
            p.v(lambda v, base=base: v.tensor_tensor(
                m_du, m_delta, m_xa[:, base:base + TC], ALU.mult))
            for n in range(NST):
                p.dma("sync", m_Br,
                      bcs[n:n + 1, :].partition_broadcast(128)[:, 0, :])
                p.dma("sync", m_Crep,
                      bcs[16 + n:17 + n, :].partition_broadcast(128)[:, 0, :])
                p.s(lambda s, dt_=dt_, n=n: s.activation(
                    m_dA, m_delta, AF.Exp,
                    scale=A_pos[:, dt_ * 16 + n:dt_ * 16 + n + 1]))
                p.v(lambda v: v.tensor_mul(m_dBu, m_du, m_Br))
                p.v(lambda v, dt_=dt_, n=n: v.tensor_tensor_scan(
                    m_h, m_dA, m_dBu,
                    m_state[:, dt_ * 16 + n:dt_ * 16 + n + 1],
                    ALU.mult, ALU.add))
                p.v(lambda v, dt_=dt_, n=n: v.tensor_copy(
                    m_state[:, dt_ * 16 + n:dt_ * 16 + n + 1],
                    m_h[:, TC - 1:TC]))
                if n == 0:
                    p.v(lambda v: v.tensor_mul(m_y, m_h, m_Crep))
                else:
                    p.v(lambda v: v.tensor_mul(m_dBu, m_h, m_Crep))
                    p.v(lambda v: v.tensor_add(m_y, m_y, m_dBu))
            p.v(lambda v, dt_=dt_, base=base: v.scalar_tensor_tensor(
                m_y, m_xa[:, base:base + TC], D_f[:, dt_:dt_ + 1], m_y,
                ALU.mult, ALU.subtract))
            p.s(lambda s, base=base: s.activation(m_sz, m_z[:, base:base + TC],
                                                  AF.Silu))
            p.v(lambda v, base=base: v.tensor_tensor(
                m_y3[:, base:base + TC], m_y, m_sz, ALU.mult))
        for ts_ in range(4):
            for kk in range(16):
                p.dma("sync", m_wk[:, 0:1024], woT[kk * 128:(kk + 1) * 128, :])
                for dv in range(2):
                    p.t(lambda t, ts_=ts_, kk=kk, dv=dv: t.matmul(
                        PS[4 + dv],
                        m_y3[:, kk * TC + ts_ * 128: kk * TC + (ts_ + 1) * 128],
                        m_wk[:, dv * 512:(dv + 1) * 512],
                        start=(kk == 0), stop=(kk == 15)))
            xs = m_x2[:, ts_ * DM:(ts_ + 1) * DM]
            p.v(lambda v, ts_=ts_, xs=xs: v.tensor_scalar(
                m_sq, xs, m_stat[:, ts_:ts_ + 1], m_stat[:, 8 + ts_:9 + ts_],
                ALU.subtract, ALU.mult))
            for dv in range(2):
                p.v(lambda v, ts_=ts_, dv=dv: v.scalar_tensor_tensor(
                    m_xi1[:, ts_ * DM + dv * 512: ts_ * DM + (dv + 1) * 512],
                    PS[4 + dv], 1.0, m_sq[:, dv * 512:(dv + 1) * 512],
                    ALU.mult, ALU.add))
        for j in range(4):
            xs = m_xi1[:, j * DM:(j + 1) * DM]
            emit_ln_stats(xs, DM, m_stat[:, j:j + 1], m_stat[:, 8 + j:9 + j],
                          m_sq)
            p.v(lambda v, j=j, xs=xs: v.tensor_scalar(
                m_u[:, j * DM:(j + 1) * DM], xs, m_stat[:, j:j + 1],
                m_stat[:, 8 + j:9 + j], ALU.subtract, ALU.mult))
        for kk in range(8):
            for j in range(4):
                p.dmat("sync",
                       m_uT[:, kk * TC + j * 128: kk * TC + (j + 1) * 128],
                       m_u[:, j * DM + kk * 128: j * DM + (kk + 1) * 128])
        for do in range(16):
            p.dma("sync", m_wk[:, 0:1024],
                  bass.AP(w1T.tensor, do * 128,
                          [[2048, 128], [262144, 8], [1, 128]]))
            for tb in range(1):
                bank = (do * 2 + tb) % 4
                for kk in range(8):
                    p.t(lambda t, kk=kk, tb=tb, bank=bank: t.matmul(
                        PS[bank], m_wk[:, kk * 128:(kk + 1) * 128],
                        m_uT[:, kk * TC + tb * 512: kk * TC + (tb + 1) * 512],
                        start=(kk == 0), stop=(kk == 7)))
                p.s(lambda s, do=do, tb=tb, bank=bank: s.activation(
                    m_gl[:, do * TC + tb * 512: do * TC + (tb + 1) * 512],
                    PS[bank], AF.Gelu, bias=bias1[:, do:do + 1]))
        for ts_ in range(4):
            for kk in range(16):
                p.dma("sync", m_wk[:, 0:1024], w2T[kk * 128:(kk + 1) * 128, :])
                for dv in range(2):
                    p.t(lambda t, ts_=ts_, kk=kk, dv=dv: t.matmul(
                        PS[4 + dv],
                        m_gl[:, kk * TC + ts_ * 128: kk * TC + (ts_ + 1) * 128],
                        m_wk[:, dv * 512:(dv + 1) * 512],
                        start=(kk == 0), stop=(kk == 15)))
            for dv in range(2):
                sl = slice(ts_ * DM + dv * 512, ts_ * DM + (dv + 1) * 512)
                p.v(lambda v, sl=sl, dv=dv: v.scalar_tensor_tensor(
                    m_x3[:, sl], PS[4 + dv], 1.0, m_xi1[:, sl],
                    ALU.mult, ALU.add))
                p.v(lambda v, sl=sl: v.tensor_add(m_x3[:, sl], m_x3[:, sl],
                                                  m_x2[:, sl]))
                p.v(lambda v, sl=sl, dv=dv: v.tensor_add(
                    m_x3[:, sl], m_x3[:, sl],
                    b2_bc[:, dv * 512:(dv + 1) * 512]))
        p.dma("sync", stream_ap(x3s, c * TC, 4), m_x3)
    sb.reset()

    if debug:
        p.dma("sync", dbg["d_x3s"], x3s)
    emit_ffn_phase(x3s, out_d, add_attn=False)

    with nc.semaphore("gsem") as sem, nc.Block() as blk:
        p.emit(blk, sem)
    return nc


def kernel(**inputs):
    x = np.ascontiguousarray(inputs["L0_attn_a_hidden_states"])
    weights = {k: np.ascontiguousarray(v) for k, v in inputs.items()
               if k != "L0_attn_a_hidden_states"}
    nc = build_program()
    in_maps = []
    for b in range(8):
        m = {"x": np.ascontiguousarray(x[b])}
        m.update(weights)
        in_maps.append(m)
    res = run_bass_kernel_spmd(nc, in_maps, core_ids=list(range(8)))
    out = np.stack([np.asarray(r["out"], dtype=np.float32)
                    for r in res.results])
    return out


# version 04
# speedup vs baseline: 3.8381x; 3.8381x over previous
import numpy as np
import concourse.bass as bass
import concourse.mybir as mybir
from concourse.bass_utils import run_bass_kernel_spmd

F32 = mybir.dt.float32
BF16 = mybir.dt.bfloat16
AF = mybir.ActivationFunctionType
ALU = mybir.AluOpType
AX = mybir.AxisListType.X

T = 4096
DM = 1024
DI = 2048
NST = 16
DTR = 64
FI = 2816
NH = 16
MAGIC = float(np.float32(3 << 22))
f32 = np.float32
_BIG = 1 << 40


SB_OFFSETS = {}
PAGE = 2048


def _ap_region(ap):
    t = ap.tensor
    esz = mybir.dt.size(ap.dtype)
    sp = ""
    try:
        sp = str(ap.space)
    except Exception:
        pass
    is_psum = "PSUM" in sp
    is_sbuf = "SBUF" in sp
    off = ap.offset
    if is_psum:
        return (t.name, 0, _BIG, _BIG)
    dims = [(int(s), int(c)) for (s, c) in list(ap.ap)]
    if not isinstance(off, int):
        return (t.name, 0, _BIG, _BIG)
    if is_sbuf:
        rowlen = int(np.prod(list(t.shape)[1:]))
        c_lo = off % rowlen
        span = 1
        for s, c in dims[1:]:
            span += abs(s) * (c - 1)
        fb = span * esz
        base = SB_OFFSETS.get(t.name)
        if base is None:
            return (t.name, c_lo * esz, (c_lo + span) * esz, fb)
        return ("@SB", base + c_lo * esz, base + (c_lo + span) * esz, fb)
    span = 1
    for s, c in dims:
        span += abs(s) * (c - 1)
    return (t.name, off * esz, (off + span) * esz, span * esz)


class Sched:
    ENGINES = ("tensor", "vector", "scalar", "gpsimd")
    TINY = 512
    GAP = 8

    def __init__(self):
        self.ops = []

    def op(self, eng, fn, r=(), w=(), barrier=False):
        self.ops.append(dict(eng=eng, fn=fn, dma=False, r=list(r), w=list(w),
                             barrier=barrier, stream=None))

    def t(self, fn, r=(), w=()): self.op("tensor", fn, r, w)
    def v(self, fn, r=(), w=()): self.op("vector", fn, r, w)
    def s(self, fn, r=(), w=()): self.op("scalar", fn, r, w)
    def g(self, fn, r=(), w=()): self.op("gpsimd", fn, r, w)

    def bar(self):
        self.op("sync", None, barrier=True)

    def dma(self, queue, stream, out, in_, **kw):
        self.ops.append(dict(eng=queue, fn=lambda e, out=out, in_=in_, kw=kw:
                             e.dma_start(out, in_, **kw),
                             dma=True, stream=stream, r=[in_], w=[out],
                             barrier=False))

    def dmat(self, queue, stream, out, in_):
        self.ops.append(dict(eng=queue, fn=lambda e, out=out, in_=in_:
                             e.dma_start_transpose(out, in_),
                             dma=True, stream=stream, r=[in_], w=[out],
                             barrier=False))

    def emit(self, nc, blk):
        ops = self.ops
        streams = []
        for o in ops:
            if o["dma"] and o["stream"] not in streams:
                streams.append(o["stream"])
        contexts = set(o["eng"] for o in ops) | {"sync"}

        eng_pos = {e: 0 for e in self.ENGINES}
        st_pos = {s: 0 for s in streams}
        writers = {}
        readers = {}
        pw = {}
        pr = {}
        barrier_toks = []
        wait_targets = {e: set() for e in self.ENGINES}

        def overlap(lst, lo, hi):
            return [rec for rec in lst if rec[0] < hi and lo < rec[1]]

        def compact(lst):
            if len(lst) <= 96:
                return
            best = {}
            lo = min(r[0] for r in lst)
            hi = max(r[1] for r in lst)
            for r in lst:
                k = (r[2], r[3])
                if k not in best or best[k][4] < r[4]:
                    best[k] = r
            lst[:] = [(lo, hi, r[2], r[3], r[4], r[5], r[6])
                      for r in best.values()]

        for o in ops:
            eng = o["eng"]
            need = {}

            def add(kind, nm, val):
                k = (kind, nm)
                if need.get(k, -1) < val:
                    need[k] = val

            if o["barrier"]:
                for e2 in self.ENGINES:
                    if eng_pos[e2]:
                        add("eng", e2, eng_pos[e2])
                for s2 in streams:
                    if st_pos[s2]:
                        add("st", s2, st_pos[s2])
                o["_need"] = dict(need)
                barrier_toks = list(need.items())
                writers.clear()
                readers.clear()
                pw.clear()
                pr.clear()
                continue

            regs_r = [_ap_region(a) for a in o["r"]]
            regs_w = [_ap_region(a) for a in o["w"]]
            mypos = (st_pos[o["stream"]] if o["dma"] else eng_pos[eng])

            def hazard(rec):
                _, _, kind, nm, val, pos2, fb2 = rec
                fb = 0
                if kind == "eng" and nm == eng and not o["dma"]:
                    if mypos - pos2 <= self.GAP and min(fb2, _fbmin[0]) < self.TINY:
                        add(kind, nm, val)
                else:
                    add(kind, nm, val)

            _fbmin = [0]
            for (tn, lo, hi, fb) in regs_r + regs_w:
                _fbmin[0] = fb
                if tn == "@SB":
                    for pg in range(lo // PAGE, (hi - 1) // PAGE + 1):
                        rec = pw.get(pg)
                        if rec is not None:
                            hazard(rec)
                else:
                    for rec in overlap(writers.get(tn, ()), lo, hi):
                        hazard(rec)
            for (tn, lo, hi, fb) in regs_w:
                _fbmin[0] = fb
                if tn == "@SB":
                    for pg in range(lo // PAGE, (hi - 1) // PAGE + 1):
                        d = pr.get(pg)
                        if d:
                            for rec in d.values():
                                hazard(rec)
                else:
                    for rec in overlap(readers.get(tn, ()), lo, hi):
                        hazard(rec)
            for (k, v) in barrier_toks:
                add(k[0], k[1], v)
            o["_need"] = dict(need)

            if o["dma"]:
                st_pos[o["stream"]] += 1
                tok = ("st", o["stream"], st_pos[o["stream"]], mypos)
            else:
                eng_pos[eng] += 1
                tok = ("eng", eng, eng_pos[eng], mypos)
            o["_tok"] = tok
            for (tn, lo, hi, fb) in regs_w:
                rec = (lo, hi, tok[0], tok[1], tok[2], tok[3], fb)
                if tn == "@SB":
                    for pg in range(lo // PAGE, (hi - 1) // PAGE + 1):
                        pw[pg] = rec
                        pr.pop(pg, None)
                else:
                    lst = writers.setdefault(tn, [])
                    lst[:] = [r2 for r2 in lst
                              if not (lo <= r2[0] and r2[1] <= hi)]
                    lst.append(rec)
                    compact(lst)
                    rl = readers.get(tn)
                    if rl:
                        rl[:] = [r2 for r2 in rl
                                 if not (lo <= r2[0] and r2[1] <= hi)]
            for (tn, lo, hi, fb) in regs_r:
                rec = (lo, hi, tok[0], tok[1], tok[2], tok[3], fb)
                if tn == "@SB":
                    for pg in range(lo // PAGE, (hi - 1) // PAGE + 1):
                        pr.setdefault(pg, {})[(tok[0], tok[1])] = rec
                else:
                    lst = readers.setdefault(tn, [])
                    lst.append(rec)
                    compact(lst)

        for o in ops:
            for (kind, nm), val in o["_need"].items():
                if kind == "eng":
                    wait_targets[nm].add(val)
        for e in self.ENGINES:
            if eng_pos[e]:
                wait_targets[e].add(eng_pos[e])
        inc_rank = {e: {pos: i + 1
                        for i, pos in enumerate(sorted(wait_targets[e]))}
                    for e in self.ENGINES}

        def xlate(kind, nm, val):
            return 16 * val if kind == "st" else inc_rank[nm][val]

        self._sem_cm = []
        sems = {}
        for name in list(self.ENGINES) + streams:
            cm = nc.semaphore(("sch_" + name)[:30])
            sems[name] = cm.__enter__()
            self._sem_cm.append(cm)

        clocks = {c: {} for c in contexts}
        per_engine = {}
        for o in ops:
            per_engine.setdefault(o["eng"], []).append(o)
        nwaits = [0]

        for eng_name, eops in per_engine.items():
            def body(e, eops=eops, eng_name=eng_name):
                ck = clocks[eng_name]
                for o in eops:
                    for (kind, nm), val in sorted(o["_need"].items()):
                        v2 = xlate(kind, nm, val)
                        if ck.get(nm, 0) < v2:
                            e.wait_ge(sems[nm], v2)
                            ck[nm] = v2
                            nwaits[0] += 1
                    if o["barrier"]:
                        continue
                    ins = o["fn"](e)
                    kind, nm, val, pos = o["_tok"]
                    if kind == "st":
                        ins.then_inc(sems[nm], 16)
                    elif val in inc_rank[nm]:
                        ins.then_inc(sems[nm], 1)
            getattr(blk, eng_name)(body)

        def fin(e):
            for e2 in self.ENGINES:
                if eng_pos[e2]:
                    e.wait_ge(sems[e2], inc_rank[e2][eng_pos[e2]])
            for s2 in streams:
                if st_pos[s2]:
                    e.wait_ge(sems[s2], 16 * st_pos[s2])
        blk.sync(fin)
        self.stats = dict(n_ops=len(ops), n_waits=nwaits[0])
        return self.stats


class SB:
    def __init__(self, nc, base=16512, limit=229344):
        self.nc = nc
        self.off = base
        self.limit = limit
        self.marks = []
        self.n = 0

    def alloc(self, name, shape, dtype, align=64):
        nbytes = int(np.prod(shape[1:])) * mybir.dt.size(dtype)
        self.off = (self.off + align - 1) // align * align
        self.n += 1
        h = self.nc.alloc_sbuf_tensor_at(f"{name}_{self.n}", list(shape), dtype,
                                         offset=self.off)
        SB_OFFSETS[f"{name}_{self.n}"] = self.off
        self.off += nbytes
        assert self.off <= self.limit, f"SBUF overflow at {name}: {self.off}"
        return h.ap()

    def mark(self):
        self.marks.append(self.off)

    def reset(self):
        self.off = self.marks.pop()


def build_program(debug=False):
    nc = bass.Bass("TRN2", target_bir_lowering=False)
    p = Sched()

    x_in = nc.dram_tensor("x", [T, DM], F32, kind="ExternalInput").ap()
    W = {}
    wshapes = {
        "L0_attn_a_w_qkv": [3 * DM, DM], "L0_attn_a_b_qkv": [3 * DM],
        "L0_attn_b_ln1_g": [DM], "L0_attn_b_ln1_b": [DM],
        "L0_attn_b_in_proj_w": [2 * DI, DM], "L0_attn_b_conv_w": [DI, 4],
        "L0_attn_b_conv_b": [DI], "L0_attn_b_x_proj_w": [96, DI],
        "L0_attn_b_dt_proj_w": [DI, DTR], "L0_attn_b_dt_proj_b": [DI],
        "L0_attn_b_A_log": [DI, NST], "L0_attn_b_D": [DI],
        "L0_attn_b_out_proj_w": [DM, DI], "L0_attn_b_ln2_g": [DM],
        "L0_attn_b_ln2_b": [DM], "L0_attn_b_mlp_w1": [2 * DM, DM],
        "L0_attn_b_mlp_b1": [2 * DM], "L0_attn_b_mlp_w2": [DM, 2 * DM],
        "L0_attn_b_mlp_b2": [DM], "L0_ffn_w_gate": [FI, DM],
        "L0_ffn_w_up": [FI, DM], "L0_ffn_w_down": [DM, FI],
    }
    for k, shp in wshapes.items():
        W[k] = nc.dram_tensor(k, shp, F32, kind="ExternalInput").ap()
    out_d = nc.dram_tensor("out", [T, DM], F32, kind="ExternalOutput").ap()

    x1s = nc.dram_tensor("x1s", [T, DM], F32, kind="Internal").ap()
    x2s = nc.dram_tensor("x2s", [T, DM], F32, kind="Internal").ap()
    x3s = nc.dram_tensor("x3s", [T, DM], F32, kind="Internal").ap()
    wqkT = nc.dram_tensor("wqkT", [DM, 2048], BF16, kind="Internal").ap()
    wvT = nc.dram_tensor("wvT", [DM, DM], BF16, kind="Internal").ap()
    wguT = nc.dram_tensor("wguT", [DM, 2 * FI], BF16, kind="Internal").ap()
    wdT = nc.dram_tensor("wdT", [FI, DM], BF16, kind="Internal").ap()
    winT = nc.dram_tensor("winT", [DM, 2 * DI], BF16, kind="Internal").ap()
    woT = nc.dram_tensor("woT", [DI, DM], BF16, kind="Internal").ap()
    w1T = nc.dram_tensor("w1T", [DM, 2 * DM], BF16, kind="Internal").ap()
    w2T = nc.dram_tensor("w2T", [2 * DM, DM], BF16, kind="Internal").ap()
    xpT = nc.dram_tensor("xpT", [DI, 96], BF16, kind="Internal").ap()
    dtT = nc.dram_tensor("dtT", [DTR, DI], BF16, kind="Internal").ap()
    bcs = nc.dram_tensor("bcs", [32, 512], BF16, kind="Internal").ap()

    def stream_ap(hbm_ap, tok0, nj):
        return bass.AP(hbm_ap.tensor, tok0 * DM,
                       [[DM, 128], [128 * DM, nj], [1, DM]])

    sb = SB(nc)
    ident = sb.alloc("ident", [128, 128], BF16)
    ones_bf = sb.alloc("ones_bf", [128, 1], BF16)
    bqk_f = sb.alloc("bqk_f", [128, 16], F32)
    bv_bc = sb.alloc("bv_bc", [128, DM], F32)
    convw = sb.alloc("convw", [128, 64], F32)
    convb = sb.alloc("convb", [128, 16], F32)
    dtbn = sb.alloc("dtbn", [128, 16], F32)
    A_pos = sb.alloc("A_pos", [128, 256], F32)
    D_f = sb.alloc("D_f", [128, 16], F32)
    bias1 = sb.alloc("bias1", [128, 16], F32)
    c0in = sb.alloc("c0in", [128, 32], F32)
    b2_bc = sb.alloc("b2_bc", [128, DM], F32)
    g1_f = sb.alloc("g1_f", [128, 8], F32)
    g2_f = sb.alloc("g2_f", [128, 8], F32)
    b1f_bf = sb.alloc("b1f_bf", [128, 8], BF16)
    b2f_bf = sb.alloc("b2f_bf", [128, 8], BF16)

    PS = [nc.alloc_psum_tensor(f"ps{i}", [128, 512], F32).ap() for i in range(8)]

    def psb(i):
        return PS[i].bitcast(BF16)

    def emit_ln_stats(xap, ncols, m_ap, r_ap, sq_scratch):
        p.s(lambda s: s.activation(sq_scratch[:, 0:ncols], xap, AF.Identity,
                                   accum_out=m_ap),
            r=[xap], w=[sq_scratch[:, 0:ncols], m_ap])
        p.s(lambda s: s.activation(sq_scratch[:, 0:ncols], xap, AF.Square,
                                   accum_out=r_ap),
            r=[xap], w=[sq_scratch[:, 0:ncols], r_ap])
        p.v(lambda v: v.tensor_scalar_mul(m_ap, m_ap, 1.0 / ncols),
            r=[m_ap], w=[m_ap])
        p.v(lambda v: v.scalar_tensor_tensor(sq_scratch[:, 0:1], m_ap, 1.0,
                                             m_ap, ALU.mult, ALU.mult),
            r=[m_ap], w=[sq_scratch[:, 0:1]])
        p.v(lambda v: v.tensor_scalar(r_ap, r_ap, 1.0 / ncols, 1e-5,
                                      ALU.mult, ALU.add), r=[r_ap], w=[r_ap])
        p.v(lambda v: v.tensor_sub(r_ap, r_ap, sq_scratch[:, 0:1]),
            r=[r_ap, sq_scratch[:, 0:1]], w=[r_ap])
        p.s(lambda s: s.activation(r_ap, r_ap, AF.Sqrt), r=[r_ap], w=[r_ap])
        p.v(lambda v: v.reciprocal(r_ap, r_ap), r=[r_ap], w=[r_ap])

    def emit_quant(src, dst_bf, ncols, bmax, sc2, s1, s2, s3):
        nb = ncols // 16
        src3 = src.rearrange("p (b k) -> p b k", k=16)
        bm = bmax[:, 0:nb]
        s2c = sc2[:, 0:nb]
        p.v(lambda v: v.reduce_max(bm, src3, axis=AX,
                                   apply_absolute_value=True),
            r=[src], w=[bm])
        p.v(lambda v: v.tensor_scalar(s2c, bm, 1.0 / 3.0, 2e-12,
                                      ALU.mult, ALU.max), r=[bm], w=[s2c])
        p.v(lambda v: v.tensor_scalar(bm, bm, 1.0 / 6.0, 1e-12,
                                      ALU.mult, ALU.max), r=[bm], w=[bm])
        p.v(lambda v: v.reciprocal(bm, bm), r=[bm], w=[bm])
        rb = bm.unsqueeze(2).broadcast_to([128, nb, 16])
        s2b = s2c.unsqueeze(2).broadcast_to([128, nb, 16])
        xn = s1[:, 0:ncols]
        xn3 = xn.rearrange("p (b k) -> p b k", k=16)
        p.v(lambda v: v.tensor_tensor(xn3, src3, rb, ALU.mult),
            r=[src, bm], w=[xn])
        y2 = src[:, 0:ncols]
        p.s(lambda s_: s_.activation(y2, xn, AF.Abs, scale=f32(2.0)),
            r=[xn], w=[y2])
        t_lo = s2[:, 0:ncols]
        p.v(lambda v: v.tensor_scalar(xn, xn, 0.0, 0.5, ALU.is_ge,
                                      ALU.subtract), r=[xn], w=[xn])
        p.v(lambda v: v.tensor_scalar(t_lo, y2, 4.0, MAGIC, ALU.min, ALU.add),
            r=[y2], w=[t_lo])
        p.v(lambda v: v.tensor_scalar(t_lo, t_lo, MAGIC, 0.5, ALU.subtract,
                                      ALU.mult), r=[t_lo], w=[t_lo])
        tmp = s3[:, 0:512]
        nseg = (ncols + 511) // 512
        for sgi in range(nseg):
            a, b = sgi * 512, min((sgi + 1) * 512, ncols)
            w_ = b - a
            tl = t_lo[:, a:b]
            y2s = y2[:, a:b]
            tmps = tmp[:, 0:w_]
            p.v(lambda v, y2s=y2s, tmps=tmps: v.tensor_scalar(
                tmps, y2s, 8.0, 0.5, ALU.min, ALU.mult),
                r=[y2s], w=[tmps])
            p.v(lambda v, tmps=tmps: v.tensor_scalar(
                tmps, tmps, MAGIC, MAGIC + 2.0, ALU.add, ALU.subtract),
                r=[tmps], w=[tmps])
            p.v(lambda v, tl=tl, tmps=tmps: v.scalar_tensor_tensor(
                tl, tmps, 0.0, tl, ALU.max, ALU.add),
                r=[tmps, tl], w=[tl])
            p.v(lambda v, y2s=y2s, tmps=tmps: v.tensor_scalar(
                tmps, y2s, 0.25, MAGIC, ALU.mult, ALU.add),
                r=[y2s], w=[tmps])
            p.v(lambda v, tmps=tmps: v.tensor_scalar(
                tmps, tmps, MAGIC + 2.0, 2.0, ALU.subtract, ALU.mult),
                r=[tmps], w=[tmps])
            p.v(lambda v, tl=tl, tmps=tmps: v.scalar_tensor_tensor(
                tl, tmps, 0.0, tl, ALU.max, ALU.add),
                r=[tmps, tl], w=[tl])
        p.v(lambda v: v.tensor_mul(t_lo, t_lo, xn), r=[t_lo, xn], w=[t_lo])
        t3 = t_lo.rearrange("p (b k) -> p b k", k=16)
        d3 = dst_bf.rearrange("p (b k) -> p b k", k=16)
        p.v(lambda v: v.tensor_tensor(d3, t3, s2b, ALU.mult),
            r=[t_lo, s2c], w=[dst_bf])

    def emit_tp_batch(tiles, pbank_lo):
        for gi in range(0, len(tiles), 8):
            grp = tiles[gi:gi + 8]
            b = pbank_lo + (gi // 8) % 2
            bview = psb(b)
            for si, (srcap, dstap) in enumerate(grp):
                sl = bview[:, si * 128:(si + 1) * 128]
                p.t(lambda t, sl=sl, srcap=srcap: t.transpose(sl, srcap, ident),
                    r=[srcap, ident], w=[PS[b]])
            for si, (srcap, dstap) in enumerate(grp):
                sl = bview[:, si * 128:(si + 1) * 128]
                p.s(lambda s, dstap=dstap, sl=sl: s.copy(dstap, sl),
                    r=[PS[b]], w=[dstap])

    p.v(lambda v: v.memset(ones_bf, 1.0), r=[], w=[ones_bf])
    p.g(lambda g: g.affine_select(ident, ones_bf.broadcast_to([128, 128]),
                                  [[-1, 128]], ALU.is_equal, 0.0, base=0,
                                  channel_multiplier=1),
        r=[ones_bf], w=[ident])
    SL = dict(allow_slow_non_contiguous=True)
    p.dma("sync", "prm", bqk_f,
          W["L0_attn_a_b_qkv"].rearrange("(c p) -> p c", p=128)[:, 0:16], **SL)
    p.dma("sync", "prm", bv_bc,
          W["L0_attn_a_b_qkv"][None, 2048:3072].partition_broadcast(128)[:, 0, :])
    p.dma("sync", "prm", convw,
          bass.AP(W["L0_attn_b_conv_w"].tensor, 0, [[4, 128], [512, 16], [1, 4]]))
    p.dma("sync", "prm", convb,
          W["L0_attn_b_conv_b"].rearrange("(c p) -> p c", p=128), **SL)
    p.dma("sync", "prm", D_f,
          W["L0_attn_b_D"].rearrange("(c p) -> p c", p=128), **SL)
    p.dma("sync", "prm", b2_bc,
          W["L0_attn_b_mlp_b2"][None, :].partition_broadcast(128)[:, 0, :])
    p.dma("sync", "prm", g1_f,
          W["L0_attn_b_ln1_g"].rearrange("(c p) -> p c", p=128), **SL)
    p.dma("sync", "prm", g2_f,
          W["L0_attn_b_ln2_g"].rearrange("(c p) -> p c", p=128), **SL)
    p.dma("sync", "prm", dtbn,
          W["L0_attn_b_dt_proj_b"].rearrange("(c p) -> p c", p=128), **SL)

    sb.mark()
    tmpb = sb.alloc("prep_tmp", [128, FI], F32)
    tbf = sb.alloc("prep_bf", [128, FI], BF16)
    tbf2 = sb.alloc("prep_bf2", [128, FI], BF16)
    qsc = sb.alloc("prep_qsc", [128, FI // 16], F32)
    qsc2 = sb.alloc("prep_qsc2", [128, FI // 16], F32)
    qs1 = sb.alloc("prep_qs1", [128, FI], F32)
    qs2 = sb.alloc("prep_qs2", [128, FI], F32)
    qs3 = sb.alloc("prep_qs3", [128, 512], F32)
    mlpb1f = sb.alloc("prep_mlpb1", [128, 16], F32)

    p.dma("sync", "prm", tmpb[:, 0:256],
          bass.AP(W["L0_attn_b_A_log"].tensor, 0, [[16, 128], [2048, 16], [1, 16]]))
    p.s(lambda s: s.activation(A_pos, tmpb[:, 0:256], AF.Exp),
        r=[tmpb[:, 0:256]], w=[A_pos])
    p.v(lambda v: v.tensor_scalar_mul(A_pos, A_pos, -1.0), r=[A_pos], w=[A_pos])
    p.dma("sync", "prm", mlpb1f,
          W["L0_attn_b_mlp_b1"].rearrange("(c p) -> p c", p=128), **SL)
    p.dma("sync", "prm", tmpb[:, 256:264],
          W["L0_attn_b_ln1_b"].rearrange("(c p) -> p c", p=128), **SL)
    p.v(lambda v: v.tensor_copy(b1f_bf, tmpb[:, 256:264]),
        r=[tmpb[:, 256:264]], w=[b1f_bf])
    p.dma("sync", "prm", tmpb[:, 264:272],
          W["L0_attn_b_ln2_b"].rearrange("(c p) -> p c", p=128), **SL)
    p.v(lambda v: v.tensor_copy(b2f_bf, tmpb[:, 264:272]),
        r=[tmpb[:, 264:272]], w=[b2f_bf])

    def prep_w(src_hbm, rows, cols, dst_hbm, quant=False, gfold=None,
               c0_rhs=None, c0_ps=None):
        rt, ct = rows // 128, cols // 128
        for r_ in range(rt):
            src = src_hbm[r_ * 128:(r_ + 1) * 128, :]
            p.dma("scalar" if r_ % 2 else "sync", "wprep2" if r_ % 2 else "wprep",
                  tmpb[:, 0:cols], src)
            if quant:
                emit_quant(tmpb[:, 0:cols], tbf[:, 0:cols], cols, qsc, qsc2,
                           qs1, qs2, qs3)
            else:
                p.v(lambda v: v.tensor_copy(tbf[:, 0:cols], tmpb[:, 0:cols]),
                    r=[tmpb[:, 0:cols]], w=[tbf[:, 0:cols]])
            emit_tp_batch([(tbf[:, c_ * 128:(c_ + 1) * 128],
                            tbf2[:, c_ * 128:(c_ + 1) * 128])
                           for c_ in range(ct)], 6)
            if c0_rhs is not None:
                for c_ in range(ct):
                    sl = tbf2[:, c_ * 128:(c_ + 1) * 128]
                    p.t(lambda t, c_=c_, r_=r_, sl=sl: t.matmul(
                        c0_ps[:, r_:r_ + 1], sl, c0_rhs[:, c_:c_ + 1],
                        start=(c_ == 0), stop=(c_ == ct - 1)),
                        r=[sl, c0_rhs], w=[c0_ps])
            if gfold is not None:
                for c_ in range(ct):
                    sl = tbf2[:, c_ * 128:(c_ + 1) * 128]
                    p.v(lambda v, sl=sl, c_=c_: v.tensor_scalar(
                        sl, sl, gfold[:, c_:c_ + 1], None, ALU.mult),
                        r=[sl, gfold[:, c_:c_ + 1]], w=[sl])
            for c_ in range(ct):
                p.dma("sync", "wprep",
                      dst_hbm[c_ * 128:(c_ + 1) * 128, r_ * 128:(r_ + 1) * 128],
                      tbf2[:, c_ * 128:(c_ + 1) * 128])

    prep_w(W["L0_attn_a_w_qkv"][0:2048, :], 2048, DM, wqkT)
    prep_w(W["L0_attn_a_w_qkv"][2048:3072, :], 1024, DM, wvT)
    prep_w(W["L0_ffn_w_gate"], FI, DM, wguT[:, 0:FI], quant=True)
    prep_w(W["L0_ffn_w_up"], FI, DM, wguT[:, FI:2 * FI], quant=True)
    prep_w(W["L0_ffn_w_down"], DM, FI, wdT, quant=True)
    prep_w(W["L0_attn_b_in_proj_w"], 2 * DI, DM, winT, gfold=g1_f,
           c0_rhs=b1f_bf, c0_ps=PS[0])
    p.v(lambda v: v.tensor_copy(c0in, PS[0][:, 0:32]),
        r=[PS[0]], w=[c0in])
    prep_w(W["L0_attn_b_mlp_w1"], 2 * DM, DM, w1T, gfold=g2_f,
           c0_rhs=b2f_bf, c0_ps=PS[1])
    p.v(lambda v: v.tensor_add(bias1, PS[1][:, 0:16], mlpb1f),
        r=[PS[1], mlpb1f], w=[bias1])
    prep_w(W["L0_attn_b_out_proj_w"], DM, DI, woT)
    prep_w(W["L0_attn_b_mlp_w2"], DM, 2 * DM, w2T)

    for r_ in range(16):
        src = W["L0_attn_b_dt_proj_w"][r_ * 128:(r_ + 1) * 128, :]
        p.dma("sync", "wprep", tmpb[:, 0:64], src)
        p.v(lambda v: v.tensor_copy(tbf[:, 0:64], tmpb[:, 0:64]),
            r=[tmpb[:, 0:64]], w=[tbf[:, 0:64]])
        p.t(lambda t: t.transpose(psb(4)[0:64, 0:128], tbf[:, 0:64], ident),
            r=[tbf[:, 0:64], ident], w=[PS[4]])
        p.s(lambda s: s.copy(tbf2[0:64, 0:128], psb(4)[0:64, 0:128]),
            r=[PS[4]], w=[tbf2[:, 0:128]])
        p.dma("sync", "wprep", dtT[:, r_ * 128:(r_ + 1) * 128],
              tbf2[0:64, 0:128])
    for c_ in range(16):
        src = W["L0_attn_b_x_proj_w"][:, c_ * 128:(c_ + 1) * 128]
        p.dma("sync", "wprep", tmpb[0:96, 0:128], src)
        p.v(lambda v: v.tensor_copy(tbf[0:96, 0:128], tmpb[0:96, 0:128]),
            r=[tmpb[:, 0:128]], w=[tbf[:, 0:128]])
        p.t(lambda t: t.transpose(psb(4)[:, 0:96], tbf[0:96, 0:128],
                                  ident[0:96, 0:96]),
            r=[tbf[:, 0:128], ident], w=[PS[4]])
        p.s(lambda s: s.copy(tbf2[:, 0:96], psb(4)[:, 0:96]),
            r=[PS[4]], w=[tbf2[:, 0:96]])
        p.dma("sync", "wprep", xpT[c_ * 128:(c_ + 1) * 128, :], tbf2[:, 0:96])
    sb.reset()

    sb.mark()
    attn_scr = sb.alloc("attn_scr", [128, 32 * 1024], BF16)
    sb.mark()
    x0t = sb.alloc("a_x0t", [128, 4 * DM], F32)
    xh = sb.alloc("a_xh", [128, 4 * DM], BF16)
    xhT = sb.alloc("a_xhT", [128, 8 * 512], BF16)
    qkf = sb.alloc("a_qkf", [128, 16 * 512], BF16)
    vtm = sb.alloc("a_v", [128, 4 * DM], BF16)
    wqk_s = sb.alloc("a_wqk", [128, 8 * 2048], BF16)
    wv_s = sb.alloc("a_wv", [128, 8 * 1024], BF16)
    Pt = sb.alloc("a_P", [128, 2 * 128], BF16)
    PTt = sb.alloc("a_PT", [128, 2 * 128], BF16)
    ao = sb.alloc("a_ao", [128, 2 * 1024], BF16)
    astat = sb.alloc("a_stat", [128, 16], F32)
    amx = sb.alloc("a_mx", [128, 8], F32)
    asum = sb.alloc("a_sume", [128, 8], F32)
    asq = sb.alloc("a_sq", [128, DM], F32)

    for kk in range(8):
        p.dma("sync", "wl", wqk_s[:, kk * 2048:(kk + 1) * 2048],
              wqkT[kk * 128:(kk + 1) * 128, :])
        p.dma("sync", "wl", wv_s[:, kk * 1024:(kk + 1) * 1024],
              wvT[kk * 128:(kk + 1) * 128, :])

    for c in range(8):
        p.dma("sync", "xin", x0t, stream_ap(x_in, c * 512, 4))
        for j in range(4):
            xs = x0t[:, j * DM:(j + 1) * DM]
            mj = astat[:, j:j + 1]
            rj = astat[:, 4 + j:5 + j]
            emit_ln_stats(xs, DM, mj, rj, asq)
            p.v(lambda v, xs=xs, mj=mj, rj=rj, j=j: v.tensor_scalar(
                xh[:, j * DM:(j + 1) * DM], xs, mj, rj,
                ALU.subtract, ALU.mult),
                r=[xs, mj, rj], w=[xh[:, j * DM:(j + 1) * DM]])
        tiles = []
        for kk in range(8):
            for j in range(4):
                tiles.append((xh[:, j * DM + kk * 128: j * DM + (kk + 1) * 128],
                              xhT[:, kk * 512 + j * 128: kk * 512 + (j + 1) * 128]))
        emit_tp_batch(tiles, 6)
        for do in range(16):
            bank = do % 4
            dst = qkf[:, do * 512:(do + 1) * 512]
            for kk in range(8):
                wsl = wqk_s[:, kk * 2048 + do * 128: kk * 2048 + (do + 1) * 128]
                msl = xhT[:, kk * 512:(kk + 1) * 512]
                p.t(lambda t, wsl=wsl, msl=msl, bank=bank, kk=kk: t.matmul(
                    PS[bank], wsl, msl, start=(kk == 0), stop=(kk == 7)),
                    r=[wsl, msl], w=[PS[bank]])
            p.s(lambda s, dst=dst, bank=bank, do=do: s.activation(
                dst, PS[bank], AF.Identity, bias=bqk_f[:, do:do + 1]),
                r=[PS[bank], bqk_f[:, do:do + 1]], w=[dst])
        for j in range(4):
            for dv in range(2):
                bank = 4 + dv
                dst = vtm[:, j * DM + dv * 512: j * DM + (dv + 1) * 512]
                for kk in range(8):
                    ssl = xhT[:, kk * 512 + j * 128: kk * 512 + (j + 1) * 128]
                    msl = wv_s[:, kk * 1024 + dv * 512: kk * 1024 + (dv + 1) * 512]
                    p.t(lambda t, ssl=ssl, msl=msl, bank=bank, kk=kk: t.matmul(
                        PS[bank], ssl, msl, start=(kk == 0), stop=(kk == 7)),
                        r=[ssl, msl], w=[PS[bank]])
                bsl = bv_bc[:, dv * 512:(dv + 1) * 512]
                p.v(lambda v, dst=dst, bank=bank, bsl=bsl: v.scalar_tensor_tensor(
                    dst, PS[bank], 1.0, bsl, ALU.mult, ALU.add),
                    r=[PS[bank], bsl], w=[dst])
        for j in range(4):
            beta = c * 4 + j
            tg, hf = beta // 2, beta % 2
            aosl_all = ao[:, (j % 2) * 1024:(j % 2) * 1024 + 1024]
            scps = {}
            for h in range(NH):
                qsl = qkf[(h % 2) * 64:(h % 2) * 64 + 64,
                          (h // 2) * 512 + j * 128:(h // 2) * 512 + (j + 1) * 128]
                ksl = qkf[(h % 2) * 64:(h % 2) * 64 + 64,
                          (8 + h // 2) * 512 + j * 128:(8 + h // 2) * 512 + (j + 1) * 128]
                bank = h % 4
                slot = h // 4
                scps[h] = PS[bank][:, slot * 128:(slot + 1) * 128]
                p.t(lambda t, qsl=qsl, ksl=ksl, sc=scps[h]: t.matmul(
                    sc, qsl, ksl, start=True, stop=True),
                    r=[qsl, ksl], w=[PS[bank]])
            for h in range(NH):
                mx0 = amx[:, h % 8:h % 8 + 1]
                p.v(lambda v, sc=scps[h], mx0=mx0: v.reduce_max(mx0, sc,
                                                                axis=AX),
                    r=[PS[h % 4]], w=[mx0])
                p.v(lambda v, mx0=mx0: v.tensor_scalar_mul(mx0, mx0,
                                                           -1.0 / 8.0),
                    r=[mx0], w=[mx0])
                su0 = asum[:, h % 8:h % 8 + 1]
                ptsl = Pt[:, (h % 2) * 128:(h % 2) * 128 + 128]
                p.s(lambda s, ptsl=ptsl, sc=scps[h], mx0=mx0, su0=su0:
                    s.activation(ptsl, sc, AF.Exp, bias=mx0,
                                 scale=f32(1.0 / 8.0), accum_out=su0),
                    r=[PS[h % 4], mx0], w=[ptsl, su0])
                p.v(lambda v, su0=su0: v.reciprocal(su0, su0),
                    r=[su0], w=[su0])
                tb = 4 + (h % 2)
                tslot = (h // 2) % 4
                tps = psb(tb)[:, tslot * 128:(tslot + 1) * 128]
                p.t(lambda t, tps=tps, ptsl=ptsl: t.transpose(tps, ptsl, ident),
                    r=[ptsl, ident], w=[PS[tb]])
                ptt = PTt[:, (h % 2) * 128:(h % 2) * 128 + 128]
                p.s(lambda s, ptt=ptt, tps=tps, tb=tb: s.copy(ptt, tps),
                    r=[PS[tb]], w=[ptt])
                vsl = vtm[:, j * DM + h * 64: j * DM + h * 64 + 64]
                pb = 6 + (h % 2)
                pslot = (h // 2) % 8
                pvps = PS[pb][:, pslot * 64:(pslot + 1) * 64]
                p.t(lambda t, pvps=pvps, ptt=ptt, vsl=vsl: t.matmul(
                    pvps, ptt, vsl, start=True, stop=True),
                    r=[ptt, vsl], w=[PS[pb]])
                aosl = aosl_all[:, h * 64:(h + 1) * 64]
                p.s(lambda s, aosl=aosl, pvps=pvps, su0=su0, pb=pb:
                    s.activation(aosl, pvps, AF.Identity, scale=su0),
                    r=[PS[pb], su0], w=[aosl])
            t0p = (16 * tg + 8 * hf) % 128
            for h in range(NH):
                tc = 2 * h + tg // 8
                dst = attn_scr[t0p:t0p + 8, tc * 1024:(tc + 1) * 1024]
                p.dma("sync", "scr", dst, aosl_all[:, h * 64:(h + 1) * 64])
    sb.reset()

    sb.mark()
    r_x0 = sb.alloc("r_x0", [128, 2 * 4 * DM], F32)
    r_x1 = sb.alloc("r_x1", [128, 2 * 4 * DM], F32)
    for c in range(8):
        par = c % 2
        x0sl = r_x0[:, par * 4 * DM:(par + 1) * 4 * DM]
        x1sl = r_x1[:, par * 4 * DM:(par + 1) * 4 * DM]
        p.dma("sync", "xin", x0sl, stream_ap(x_in, c * 512, 4))
        for j in range(4):
            asl = attn_scr[:, (c * 4 + j) * 1024:(c * 4 + j + 1) * 1024]
            xa_ = x0sl[:, j * DM:(j + 1) * DM]
            xo_ = x1sl[:, j * DM:(j + 1) * DM]
            p.v(lambda v, asl=asl, xa_=xa_, xo_=xo_: v.scalar_tensor_tensor(
                xo_, asl, 1.0, xa_, ALU.mult, ALU.add),
                r=[asl, xa_], w=[xo_])
        p.dma("sync", "xout", stream_ap(x1s, c * 512, 4), x1sl)
    sb.reset()
    sb.reset()

    def emit_ffn_phase(src_hbm, dst_hbm):
        sb.mark()
        x1t2 = sb.alloc("f_x1t", [128, 2 * 4 * DM], F32)
        xqh = sb.alloc("f_xh", [128, DM], F32)
        xq = sb.alloc("f_xq", [128, 4 * DM], BF16)
        xqT = sb.alloc("f_xqT", [128, 8 * 512], BF16)
        sg = sb.alloc("f_sg", [128, 4 * FI], BF16)
        hh = sb.alloc("f_h", [128, 4 * FI], F32)
        hq = sb.alloc("f_hq", [128, FI], BF16)
        hqT = sb.alloc("f_hqT", [128, 2 * 22 * 128], BF16)
        x2t = sb.alloc("f_x2t", [128, 4 * DM], F32)
        fstat = sb.alloc("f_stat", [128, 16], F32)
        fsq = sb.alloc("f_sq", [128, DM], F32)
        bmax = sb.alloc("f_bmax", [128, FI // 16], F32)
        sc2 = sb.alloc("f_sc2", [128, FI // 16], F32)
        fq1 = sb.alloc("f_q1", [128, FI], F32)
        fq2 = sb.alloc("f_q2", [128, FI], F32)
        fq3 = sb.alloc("f_q3", [128, 512], F32)
        wgu = sb.alloc("f_wgu", [128, 2 * 1024], BF16)
        wd_s = sb.alloc("f_wd", [128, 2 * DM], BF16)

        groups = [(0, 1), (2, 3), (4, 5), (6, 7), (8, 9), (10,)]
        for c in range(8):
            x1t = x1t2[:, (c % 2) * 4 * DM:(c % 2 + 1) * 4 * DM]
            p.dma("sync", "xin", x1t, stream_ap(src_hbm, c * 512, 4))
            for j in range(4):
                xs = x1t[:, j * DM:(j + 1) * DM]
                mj = fstat[:, j:j + 1]
                rj = fstat[:, 4 + j:5 + j]
                emit_ln_stats(xs, DM, mj, rj, fsq)
            for j in range(4):
                xs = x1t[:, j * DM:(j + 1) * DM]
                mj = fstat[:, j:j + 1]
                rj = fstat[:, 4 + j:5 + j]
                p.v(lambda v, xs=xs, mj=mj, rj=rj: v.tensor_scalar(
                    xqh, xs, mj, rj, ALU.subtract, ALU.mult),
                    r=[xs, mj, rj], w=[xqh])
                emit_quant(xqh, xq[:, j * DM:(j + 1) * DM], DM, bmax, sc2,
                           fq1, fq2, fq3)
            tiles = []
            for kk in range(8):
                for j in range(4):
                    tiles.append((xq[:, j * DM + kk * 128: j * DM + (kk + 1) * 128],
                                  xqT[:, kk * 512 + j * 128: kk * 512 + (j + 1) * 128]))
            emit_tp_batch(tiles, 6)
            for gi, grp in enumerate(groups):
                ng = len(grp)
                for kk in range(8):
                    wsl = wgu[:, (kk % 2) * 1024:(kk % 2) * 1024 + ng * 512]
                    p.dma("sync", "wl", wsl,
                          wguT[kk * 128:(kk + 1) * 128,
                               grp[0] * 512:(grp[0] + ng) * 512])
                    for j in range(4):
                        ssl = xqT[:, kk * 512 + j * 128: kk * 512 + (j + 1) * 128]
                        for m in range(ng):
                            bank = j * 2 + m
                            msl = wsl[:, m * 512:(m + 1) * 512]
                            p.t(lambda t, ssl=ssl, msl=msl, bank=bank, kk=kk:
                                t.matmul(PS[bank], ssl, msl,
                                         start=(kk == 0), stop=(kk == 7)),
                                r=[ssl, msl], w=[PS[bank]])
                for j in range(4):
                    for m in range(ng):
                        dt_ = grp[m]
                        bank = j * 2 + m
                        if dt_ < 5:
                            dst = sg[:, j * FI + dt_ * 512: j * FI + (dt_ + 1) * 512]
                            p.s(lambda s, dst=dst, bank=bank: s.activation(
                                dst, PS[bank], AF.Silu), r=[PS[bank]], w=[dst])
                        elif dt_ == 5:
                            dst = sg[:, j * FI + 2560: j * FI + 2816]
                            p.s(lambda s, dst=dst, bank=bank: s.activation(
                                dst, PS[bank][:, 0:256], AF.Silu),
                                r=[PS[bank]], w=[dst])
                            d2 = hh[:, j * FI: j * FI + 256]
                            s2_ = sg[:, j * FI: j * FI + 256]
                            p.v(lambda v, d2=d2, bank=bank, s2_=s2_:
                                v.tensor_tensor(d2, PS[bank][:, 256:512], s2_,
                                                ALU.mult),
                                r=[PS[bank], s2_], w=[d2])
                        else:
                            u0 = dt_ * 512 - FI
                            d2 = hh[:, j * FI + u0: j * FI + u0 + 512]
                            s2_ = sg[:, j * FI + u0: j * FI + u0 + 512]
                            p.v(lambda v, d2=d2, bank=bank, s2_=s2_:
                                v.tensor_tensor(d2, PS[bank], s2_, ALU.mult),
                                r=[PS[bank], s2_], w=[d2])
            for jp in range(2):
                for j in (2 * jp, 2 * jp + 1):
                    emit_quant(hh[:, j * FI:(j + 1) * FI], hq, FI, bmax, sc2,
                               fq1, fq2, fq3)
                    tiles = [(hq[:, kk * 128:(kk + 1) * 128],
                              hqT[:, (j % 2) * 2816 + kk * 128:
                                  (j % 2) * 2816 + (kk + 1) * 128])
                             for kk in range(22)]
                    emit_tp_batch(tiles, 6)
                for kk in range(22):
                    wsl = wd_s[:, (kk % 2) * 1024:(kk % 2) * 1024 + 1024]
                    p.dma("sync", "wl", wsl, wdT[kk * 128:(kk + 1) * 128, :])
                    for j2 in range(2):
                        ssl = hqT[:, j2 * 2816 + kk * 128:
                                  j2 * 2816 + (kk + 1) * 128]
                        for dv in range(2):
                            bank = (jp * 2 + j2) * 2 + dv
                            msl = wsl[:, dv * 512:(dv + 1) * 512]
                            p.t(lambda t, ssl=ssl, msl=msl, bank=bank, kk=kk:
                                t.matmul(PS[bank], ssl, msl,
                                         start=(kk == 0), stop=(kk == 21)),
                                r=[ssl, msl], w=[PS[bank]])
            for j in range(4):
                for dv in range(2):
                    bank = j * 2 + dv
                    sl_o = x2t[:, j * DM + dv * 512: j * DM + (dv + 1) * 512]
                    sl_i = x1t[:, j * DM + dv * 512: j * DM + (dv + 1) * 512]
                    p.v(lambda v, sl_o=sl_o, sl_i=sl_i, bank=bank:
                        v.scalar_tensor_tensor(sl_o, PS[bank], 1.0, sl_i,
                                               ALU.mult, ALU.add),
                        r=[PS[bank], sl_i], w=[sl_o])
            p.dma("sync", "xout", stream_ap(dst_hbm, c * 512, 4), x2t)
        sb.reset()

    emit_ffn_phase(x1s, x2s)

    sb.mark()
    TC = 512
    m_x2 = sb.alloc("m_x2", [128, 4 * DM], F32)
    m_xi1 = sb.alloc("m_xi1", [128, 4 * DM], F32)
    m_R1 = sb.alloc("m_R1", [128, 16 * TC], BF16)
    m_R2 = sb.alloc("m_R2", [128, 16 * TC], BF16)
    m_z = sb.alloc("m_z", [128, 16 * TC], BF16)
    m_uu = sb.alloc("m_uu", [128, 16 * TC], BF16)
    m_wk = sb.alloc("m_wk", [128, 2 * 1024], BF16)
    m_xdbl = sb.alloc("m_xdbl", [128, TC], F32)
    m_stat = sb.alloc("m_stat", [128, 32], F32)
    m_sq = sb.alloc("m_sq", [128, DM], F32)
    m_state = sb.alloc("m_state", [128, 256], F32)
    m_tail = sb.alloc("m_tail", [128, 64], BF16)
    m_xc = sb.alloc("m_xc", [128, TC], F32)
    m_delta = sb.alloc("m_delta", [128, 4 * TC], BF16)
    m_sgt = sb.alloc("m_sgt", [128, 2 * TC], F32)
    m_du = sb.alloc("m_du", [128, 2 * TC], BF16)
    m_BC = sb.alloc("m_BC", [128, 32 * TC], BF16)
    m_dA = sb.alloc("m_dA", [128, 8 * TC], BF16)
    m_dBu = sb.alloc("m_dBu", [128, 8 * TC], BF16)
    m_hh = sb.alloc("m_hh", [128, 8 * TC], BF16)
    m_pr = sb.alloc("m_pr", [128, 8 * TC], BF16)
    m_yy = sb.alloc("m_yy", [128, 2 * TC], F32)
    m_xpa = sb.alloc("m_xpa", [128, 16 * 96], BF16)
    m_dta = sb.alloc("m_dta", [128, 2048], BF16)

    m_u = m_uu[:, 0:4 * DM]
    m_uT = m_uu[:, 4 * DM:8 * DM]
    m_xm = m_R1
    m_xa = m_R1
    m_y3 = m_R2
    m_gl = m_R2
    m_x3 = m_R1.bitcast(F32)
    m_xcb = m_xc.bitcast(BF16)
    m_sz = m_xcb[:, 0:TC]
    m_dtb = m_xcb[:, TC:2 * TC]

    p.v(lambda v: v.memset(m_state, 0.0), r=[], w=[m_state])
    p.v(lambda v: v.memset(m_tail, 0.0), r=[], w=[m_tail])
    p.dma("sync", "wl", m_xpa,
          bass.AP(xpT.tensor, 0, [[96, 128], [12288, 16], [1, 96]]))
    p.dma("sync", "wl", m_dta[0:64, :], dtT)

    NCH = T // TC
    for c in range(NCH):
        p.dma("sync", "xin", m_x2, stream_ap(x2s, c * TC, 4))
        for j in range(4):
            xs = m_x2[:, j * DM:(j + 1) * DM]
            mj = m_stat[:, j:j + 1]
            r1 = m_stat[:, 8 + j:9 + j]
            t1 = m_stat[:, 16 + j:17 + j]
            emit_ln_stats(xs, DM, mj, r1, m_sq)
            p.v(lambda v, r1=r1, t1=t1: v.tensor_tensor(t1, r1, r1, ALU.mult),
                r=[r1], w=[t1])
            p.v(lambda v, t1=t1: v.tensor_scalar(t1, t1, -1e-5, 1.0 + 1e-5,
                                                 ALU.mult, ALU.add),
                r=[t1], w=[t1])
            p.s(lambda s, t1=t1: s.activation(t1, t1, AF.Sqrt), r=[t1], w=[t1])
            p.v(lambda v, t1=t1: v.reciprocal(t1, t1), r=[t1], w=[t1])
            p.v(lambda v, r1=r1, t1=t1: v.tensor_tensor(t1, t1, r1, ALU.mult),
                r=[t1, r1], w=[t1])
            usl = m_u[:, j * DM:(j + 1) * DM]
            p.v(lambda v, xs=xs, mj=mj, t1=t1, usl=usl: v.tensor_scalar(
                usl, xs, mj, t1, ALU.subtract, ALU.mult),
                r=[xs, mj, t1], w=[usl])
        tiles = []
        for kk in range(8):
            for j in range(4):
                tiles.append((m_u[:, j * DM + kk * 128: j * DM + (kk + 1) * 128],
                              m_uT[:, kk * TC + j * 128: kk * TC + (j + 1) * 128]))
        emit_tp_batch(tiles, 6)
        for do in range(32):
            wsl = m_wk[:, (do % 2) * 1024:(do % 2) * 1024 + 1024]
            p.dma("sync", "wl", wsl,
                  bass.AP(winT.tensor, do * 128,
                          [[4096, 128], [524288, 8], [1, 128]]))
            bank = do % 4
            for kk in range(8):
                s_ = wsl[:, kk * 128:(kk + 1) * 128]
                m_ = m_uT[:, kk * TC:(kk + 1) * TC]
                p.t(lambda t, s_=s_, m_=m_, bank=bank, kk=kk: t.matmul(
                    PS[bank], s_, m_, start=(kk == 0), stop=(kk == 7)),
                    r=[s_, m_], w=[PS[bank]])
            dst = (m_xm if do < 16 else m_z)
            dsl = dst[:, (do % 16) * TC:(do % 16 + 1) * TC]
            p.s(lambda s, dsl=dsl, bank=bank, do=do: s.activation(
                dsl, PS[bank], AF.Identity, bias=c0in[:, do:do + 1]),
                r=[PS[bank], c0in[:, do:do + 1]], w=[dsl])
        for dt_ in range(16):
            base = dt_ * TC
            xmsl = m_xm[:, base:base + TC]
            cw3 = convw[:, dt_ * 4 + 3:dt_ * 4 + 4]
            cb = convb[:, dt_:dt_ + 1]
            p.v(lambda v, xmsl=xmsl, cw3=cw3, cb=cb: v.tensor_scalar(
                m_xc, xmsl, cw3, cb, ALU.mult, ALU.add),
                r=[xmsl, cw3, cb], w=[m_xc])
            for k in range(3):
                sh = 3 - k
                cwk = convw[:, dt_ * 4 + k:dt_ * 4 + k + 1]
                src_ = m_xm[:, base:base + TC - sh]
                p.v(lambda v, src_=src_, cwk=cwk, sh=sh: v.scalar_tensor_tensor(
                    m_xc[:, sh:TC], src_, cwk, m_xc[:, sh:TC],
                    ALU.mult, ALU.add),
                    r=[src_, cwk, m_xc[:, sh:TC]], w=[m_xc[:, sh:TC]])
                for i in range(sh):
                    tsl = m_tail[:, dt_ * 4 + k + i: dt_ * 4 + k + i + 1]
                    p.v(lambda v, tsl=tsl, cwk=cwk, i=i: v.scalar_tensor_tensor(
                        m_xc[:, i:i + 1], tsl, cwk, m_xc[:, i:i + 1],
                        ALU.mult, ALU.add),
                        r=[tsl, cwk, m_xc[:, i:i + 1]], w=[m_xc[:, i:i + 1]])
            tl = m_tail[:, dt_ * 4:dt_ * 4 + 3]
            p.v(lambda v, tl=tl, xmsl=xmsl: v.tensor_copy(
                tl, xmsl[:, TC - 3:TC]), r=[xmsl], w=[tl])
            xasl = m_xa[:, base:base + TC]
            p.s(lambda s, xasl=xasl: s.activation(xasl, m_xc, AF.Silu),
                r=[m_xc], w=[xasl])
        p.s(lambda s: s.activation(m_z, m_z, AF.Silu), r=[m_z], w=[m_z])
        for kk in range(16):
            s_ = m_xpa[:, kk * 96:(kk + 1) * 96]
            m_ = m_xa[:, kk * TC:(kk + 1) * TC]
            p.t(lambda t, s_=s_, m_=m_, kk=kk: t.matmul(
                PS[4][0:96, :], s_, m_, start=(kk == 0), stop=(kk == 15)),
                r=[s_, m_], w=[PS[4]])
        p.v(lambda v: v.tensor_copy(m_xdbl[0:96, :], PS[4][0:96, :]),
            r=[PS[4]], w=[m_xdbl])
        p.v(lambda v: v.tensor_copy(m_dtb[0:64, :], m_xdbl[0:64, :]),
            r=[m_xdbl], w=[m_dtb])
        p.v(lambda v: v.tensor_copy(m_sz[0:32, :], m_xdbl[64:96, :]),
            r=[m_xdbl], w=[m_sz])
        p.dma("sync", "bc", bcs, m_sz[0:32, :])
        p.dma("sync", "bc", m_BC[:, 0:16 * TC],
              bcs[0:16, :].partition_broadcast(128).rearrange("p a b -> p (a b)"))
        p.dma("sync", "bc", m_BC[:, 16 * TC:32 * TC],
              bcs[16:32, :].partition_broadcast(128).rearrange("p a b -> p (a b)"))
        for dt_ in range(16):
            base = dt_ * TC
            dusl = m_du[:, (dt_ % 2) * TC:(dt_ % 2 + 1) * TC]
            dsl = m_delta[:, (dt_ % 4) * TC:(dt_ % 4 + 1) * TC]
            xasl = m_xa[:, base:base + TC]
            ybank = 4 + (dt_ % 2)
            dpb = dt_ % 4
            s_ = m_dta[0:64, dt_ * 128:(dt_ + 1) * 128]
            p.t(lambda t, s_=s_, dpb=dpb: t.matmul(
                PS[dpb], s_, m_dtb[0:64, :], start=True, stop=True),
                r=[s_, m_dtb], w=[PS[dpb]])
            ri2 = dt_ % 2
            a1 = m_sgt[:, ri2 * TC:(ri2 + 1) * TC]
            bsl = dtbn[:, dt_:dt_ + 1]
            p.s(lambda s, a1=a1, dpb=dpb, bsl=bsl: s.activation(
                a1, PS[dpb], AF.Abs, bias=bsl),
                r=[PS[dpb], bsl], w=[a1])
            p.s(lambda s, a1=a1: s.activation(a1, a1, AF.Exp,
                                              scale=f32(-1.0)),
                r=[a1], w=[a1])
            p.s(lambda s, a1=a1: s.activation(a1, a1, AF.Ln,
                                              bias=f32(1.0)),
                r=[a1], w=[a1])
            a2 = m_yy[:, ri2 * TC:(ri2 + 1) * TC]
            p.v(lambda v, a2=a2, dpb=dpb, bsl=bsl: v.tensor_scalar(
                a2, PS[dpb], bsl, 0.0, ALU.add, ALU.max),
                r=[PS[dpb], bsl], w=[a2])
            p.v(lambda v, dsl=dsl, a1=a1, a2=a2: v.tensor_add(dsl, a2, a1),
                r=[a1, a2], w=[dsl])
            p.v(lambda v, dusl=dusl, dsl=dsl, xasl=xasl: v.tensor_mul(
                dusl, dsl, xasl), r=[dsl, xasl], w=[dusl])
            for n in range(NST):
                ri = n % 8
                dA = m_dA[:, ri * TC:(ri + 1) * TC]
                dBu = m_dBu[:, ri * TC:(ri + 1) * TC]
                hsl = m_hh[:, ri * TC:(ri + 1) * TC]
                prs = m_pr[:, ri * TC:(ri + 1) * TC]
                Asl = A_pos[:, dt_ * 16 + n:dt_ * 16 + n + 1]
                Bsl = m_BC[:, n * TC:(n + 1) * TC]
                Csl = m_BC[:, (16 + n) * TC:(17 + n) * TC]
                stsl = m_state[:, dt_ * 16 + n:dt_ * 16 + n + 1]
                p.s(lambda s, dA=dA, dsl=dsl, Asl=Asl: s.activation(
                    dA, dsl, AF.Exp, scale=Asl), r=[dsl, Asl], w=[dA])
                p.v(lambda v, dBu=dBu, dusl=dusl, Bsl=Bsl: v.tensor_mul(
                    dBu, dusl, Bsl), r=[dusl, Bsl], w=[dBu])
                p.v(lambda v, hsl=hsl, dA=dA, dBu=dBu, stsl=stsl:
                    v.tensor_tensor_scan(hsl, dA, dBu, stsl, ALU.mult,
                                         ALU.add),
                    r=[dA, dBu, stsl], w=[hsl])
                if n % 4 == 3:
                    st4 = m_state[:, dt_ * 16 + n - 3:dt_ * 16 + n + 1]
                    hh4 = bass.AP(m_hh.tensor, ((n - 3) % 8) * TC + TC - 1,
                                  [[8 * TC, 128], [TC, 4]])
                    p.v(lambda v, st4=st4, hh4=hh4: v.tensor_copy(st4, hh4),
                        r=[m_hh], w=[st4])
                p.v(lambda v, prs=prs, hsl=hsl, Csl=Csl: v.tensor_mul(
                    prs, hsl, Csl), r=[hsl, Csl], w=[prs])
                p.t(lambda t, prs=prs, ybank=ybank, n=n: t.matmul(
                    PS[ybank], ident, prs, start=(n == 0), stop=(n == NST - 1)),
                    r=[prs, ident], w=[PS[ybank]])
            xasl2 = m_xa[:, base:base + TC]
            zsl = m_z[:, base:base + TC]
            yy = m_yy[:, (dt_ % 2) * TC:(dt_ % 2 + 1) * TC]
            p.v(lambda v, yy=yy, xasl2=xasl2, ybank=ybank, dt_=dt_:
                v.scalar_tensor_tensor(yy, xasl2, D_f[:, dt_:dt_ + 1],
                                       PS[ybank], ALU.mult, ALU.add),
                r=[xasl2, D_f[:, dt_:dt_ + 1], PS[ybank]], w=[yy])
            y3sl = m_y3[:, base:base + TC]
            p.v(lambda v, y3sl=y3sl, yy=yy, zsl=zsl: v.tensor_tensor(
                y3sl, yy, zsl, ALU.mult), r=[yy, zsl], w=[y3sl])
        for kk in range(16):
            wsl = m_wk[:, (kk % 2) * 1024:(kk % 2) * 1024 + 1024]
            p.dma("sync", "wl", wsl, woT[kk * 128:(kk + 1) * 128, :])
            for ts_ in range(4):
                ssl = m_y3[:, kk * TC + ts_ * 128: kk * TC + (ts_ + 1) * 128]
                for dv in range(2):
                    bank = ts_ * 2 + dv
                    msl = wsl[:, dv * 512:(dv + 1) * 512]
                    p.t(lambda t, ssl=ssl, msl=msl, bank=bank, kk=kk: t.matmul(
                        PS[bank], ssl, msl, start=(kk == 0), stop=(kk == 15)),
                        r=[ssl, msl], w=[PS[bank]])
        for ts_ in range(4):
            xs = m_x2[:, ts_ * DM:(ts_ + 1) * DM]
            p.v(lambda v, xs=xs, ts_=ts_: v.tensor_scalar(
                m_sq, xs, m_stat[:, ts_:ts_ + 1], m_stat[:, 8 + ts_:9 + ts_],
                ALU.subtract, ALU.mult),
                r=[xs, m_stat[:, ts_:ts_ + 1], m_stat[:, 8 + ts_:9 + ts_]],
                w=[m_sq])
            for dv in range(2):
                bank = ts_ * 2 + dv
                osl = m_xi1[:, ts_ * DM + dv * 512: ts_ * DM + (dv + 1) * 512]
                p.v(lambda v, osl=osl, bank=bank, dv=dv: v.scalar_tensor_tensor(
                    osl, PS[bank], 1.0, m_sq[:, dv * 512:(dv + 1) * 512],
                    ALU.mult, ALU.add),
                    r=[PS[bank], m_sq[:, dv * 512:(dv + 1) * 512]], w=[osl])
        for j in range(4):
            xs = m_xi1[:, j * DM:(j + 1) * DM]
            mj = m_stat[:, j:j + 1]
            rj = m_stat[:, 8 + j:9 + j]
            emit_ln_stats(xs, DM, mj, rj, m_sq)
            usl = m_u[:, j * DM:(j + 1) * DM]
            p.v(lambda v, xs=xs, mj=mj, rj=rj, usl=usl: v.tensor_scalar(
                usl, xs, mj, rj, ALU.subtract, ALU.mult),
                r=[xs, mj, rj], w=[usl])
        tiles = []
        for kk in range(8):
            for j in range(4):
                tiles.append((m_u[:, j * DM + kk * 128: j * DM + (kk + 1) * 128],
                              m_uT[:, kk * TC + j * 128: kk * TC + (j + 1) * 128]))
        emit_tp_batch(tiles, 6)
        for do in range(16):
            wsl = m_wk[:, (do % 2) * 1024:(do % 2) * 1024 + 1024]
            p.dma("sync", "wl", wsl,
                  bass.AP(w1T.tensor, do * 128,
                          [[2048, 128], [262144, 8], [1, 128]]))
            bank = do % 4
            for kk in range(8):
                s_ = wsl[:, kk * 128:(kk + 1) * 128]
                m_ = m_uT[:, kk * TC:(kk + 1) * TC]
                p.t(lambda t, s_=s_, m_=m_, bank=bank, kk=kk: t.matmul(
                    PS[bank], s_, m_, start=(kk == 0), stop=(kk == 7)),
                    r=[s_, m_], w=[PS[bank]])
            gsl = m_gl[:, do * TC:(do + 1) * TC]
            p.s(lambda s, gsl=gsl, bank=bank, do=do: s.activation(
                gsl, PS[bank], AF.Gelu, bias=bias1[:, do:do + 1]),
                r=[PS[bank], bias1[:, do:do + 1]], w=[gsl])
        for kk in range(16):
            wsl = m_wk[:, (kk % 2) * 1024:(kk % 2) * 1024 + 1024]
            p.dma("sync", "wl", wsl, w2T[kk * 128:(kk + 1) * 128, :])
            for ts_ in range(4):
                ssl = m_gl[:, kk * TC + ts_ * 128: kk * TC + (ts_ + 1) * 128]
                for dv in range(2):
                    bank = ts_ * 2 + dv
                    msl = wsl[:, dv * 512:(dv + 1) * 512]
                    p.t(lambda t, ssl=ssl, msl=msl, bank=bank, kk=kk: t.matmul(
                        PS[bank], ssl, msl, start=(kk == 0), stop=(kk == 15)),
                        r=[ssl, msl], w=[PS[bank]])
        for ts_ in range(4):
            for dv in range(2):
                bank = ts_ * 2 + dv
                sl = slice(ts_ * DM + dv * 512, ts_ * DM + (dv + 1) * 512)
                x3sl = m_x3[:, sl]
                xi1sl = m_xi1[:, sl]
                x2sl = m_x2[:, sl]
                b2sl = b2_bc[:, dv * 512:(dv + 1) * 512]
                p.v(lambda v, x3sl=x3sl, xi1sl=xi1sl, bank=bank:
                    v.scalar_tensor_tensor(x3sl, PS[bank], 1.0, xi1sl,
                                           ALU.mult, ALU.add),
                    r=[PS[bank], xi1sl], w=[x3sl])
                p.v(lambda v, x3sl=x3sl, x2sl=x2sl: v.tensor_add(
                    x3sl, x3sl, x2sl), r=[x3sl, x2sl], w=[x3sl])
                p.v(lambda v, x3sl=x3sl, b2sl=b2sl: v.tensor_add(
                    x3sl, x3sl, b2sl), r=[x3sl, b2sl], w=[x3sl])
        p.dma("sync", "xout", stream_ap(x3s, c * TC, 4), m_x3)
    sb.reset()

    emit_ffn_phase(x3s, out_d)

    with nc.Block() as blk:
        p.emit(nc, blk)
    return nc


def kernel(**inputs):
    x = np.ascontiguousarray(inputs["L0_attn_a_hidden_states"])
    weights = {k: np.ascontiguousarray(v) for k, v in inputs.items()
               if k != "L0_attn_a_hidden_states"}
    nc = build_program()
    in_maps = []
    for b in range(8):
        m = {"x": np.ascontiguousarray(x[b])}
        m.update(weights)
        in_maps.append(m)
    res = run_bass_kernel_spmd(nc, in_maps, core_ids=list(range(8)))
    out = np.stack([np.asarray(r["out"], dtype=np.float32)
                    for r in res.results])
    return out
